# version 49
# baseline (speedup 1.0000x reference)
import numpy as np
import ml_dtypes
from collections import defaultdict

import concourse.bass as bass
import concourse.mybir as mybir
from concourse.bass_utils import run_bass_kernel_spmd

F32 = mybir.dt.float32
BF16 = mybir.dt.bfloat16
I32 = mybir.dt.int32
I16 = mybir.dt.int16
AF = mybir.ActivationFunctionType
ALU = mybir.AluOpType
AX = mybir.AxisListType

BB, SS, EE = 2, 8192, 2048
H, HKV, DH = 16, 4, 128
NMLP = 16384
BLK = 512
NB = SS // BLK
ECH = EE // 128
SCALE = DH ** -0.5
GSH = 2048
DSH = 256
SH_G = 2048 * 2048
SH_D = 16384 * 256
SH = SH_G * 2 + SH_D
CH = SH // 4

ENGINES = ("sync", "scalar", "vector", "tensor", "gpsimd")


def _key_of(x):
    if isinstance(x, str):
        return x
    t = getattr(x, "tensor", None)
    if t is not None:
        return t.name
    return x.name


class _Chan:
    __slots__ = ("name", "queue", "sem", "count", "mode", "key")

    def __init__(self, nc, name, queue, mode):
        self.name = name
        self.queue = queue
        self.sem = nc.alloc_semaphore(name=f"ch_{name}")
        self.count = 0
        self.mode = mode
        self.key = None


class _Tok:
    __slots__ = ("eng", "sem", "val", "chan")

    def __init__(self, eng, sem=None, val=0, chan=None):
        self.eng = eng
        self.sem = sem
        self.val = val
        self.chan = chan

    def resolve(self):
        if self.chan is not None:
            if self.chan.mode == "bulk":
                return self.chan.sem, 16 * self.chan.count
            return self.chan.sem, 16 * self.val
        return self.sem, self.val


class _Eng:
    def __init__(self, nc, name):
        self.name = name
        self.sem = nc.alloc_semaphore(name=f"sem_{name}")
        self.count = 0
        self.waited = defaultdict(int)
        self.items = []


class Prog:
    def __init__(self, nc):
        self.nc = nc
        self.eng = {n: _Eng(nc, n) for n in ENGINES}
        self.chans = {}
        self.last_w = {}
        self.readers = defaultdict(list)
        self.finalized = False

    def chan(self, name, queue, mode="ring"):
        c = _Chan(self.nc, name, queue, mode)
        self.chans[name] = c
        return c

    def _wait(self, e, tok, waits):
        sem, val = tok.resolve()
        if e.waited[id(sem)] < val:
            waits.append((sem, val))
            e.waited[id(sem)] = val

    def _deps(self, engname, e, rkeys, wkeys, pe_relaxed):
        waits = []
        for k in rkeys:
            tok = self.last_w.get(k)
            if tok is not None and not (pe_relaxed and tok.eng == engname == "tensor"):
                self._wait(e, tok, waits)
        for k in wkeys:
            tok = self.last_w.get(k)
            if tok is not None and not (pe_relaxed and tok.eng == engname == "tensor"):
                self._wait(e, tok, waits)
            for rtok in self.readers.get(k, ()):
                if pe_relaxed and rtok.eng == engname == "tensor":
                    continue
                self._wait(e, rtok, waits)
        return waits

    def _record(self, tok, rkeys, wkeys):
        for k in wkeys:
            self.last_w[k] = tok
            self.readers[k] = []
        for k in rkeys:
            self.readers[k].append(tok)

    def op(self, engname, fn, reads=None, writes=None, pe_relaxed=True):
        return self.op_group(engname, [fn], reads, writes, pe_relaxed)

    def op_group(self, engname, fns, reads=None, writes=None, pe_relaxed=True):
        assert not self.finalized
        e = self.eng[engname]
        rkeys = [_key_of(r) for r in (reads or [])]
        wkeys = [_key_of(w) for w in (writes or [])]
        waits = self._deps(engname, e, rkeys, wkeys, pe_relaxed)
        e.count += 1
        tok = _Tok(engname, sem=e.sem, val=e.count)
        self._record(tok, rkeys, wkeys)
        n = len(fns)
        for i, fn in enumerate(fns):
            if i == n - 1:
                e.items.append((fn, waits if i == 0 else (), e.sem, 1))
            else:
                e.items.append((fn, waits if i == 0 else (), None, 0))
        return tok

    def dma(self, ch, out, in_, reads=None, writes=None, **kw):
        assert not self.finalized
        if isinstance(ch, str):
            ch = self.chans[ch]
        e = self.eng[ch.queue]
        rkeys = [_key_of(r) for r in (reads if reads is not None else [in_])]
        wkeys = [_key_of(w) for w in (writes if writes is not None else [out])]
        if ch.mode == "ring":
            k = tuple(sorted(wkeys))
            if ch.key is None:
                ch.key = k
            else:
                assert ch.key == k, f"ring chan {ch.name}: {k} != {ch.key}"
        waits = self._deps(ch.queue, e, rkeys, wkeys, pe_relaxed=False)
        ch.count += 1
        tok = _Tok(ch.queue, chan=ch, val=ch.count)
        self._record(tok, rkeys, wkeys)

        def fn(eng, _o=out, _i=in_, _kw=kw):
            return eng.dma_start(out=_o, in_=_i, **_kw)

        e.items.append((fn, waits, ch.sem, 16))
        return tok

    def wait_tokens(self, engname, tokens):
        e = self.eng[engname]
        waits = []
        for tok in tokens:
            self._wait(e, tok, waits)
        if waits:
            e.items.append((None, waits, None, 0))

    def wait_keys(self, engname, keys):
        toks = [self.last_w[_key_of(k)] for k in keys if _key_of(k) in self.last_w]
        self.wait_tokens(engname, toks)

    def barrier_all(self, limit=None):
        toks = []
        for n in ENGINES:
            e = self.eng[n]
            cnt = min(e.count, limit.get(n, e.count)) if limit else e.count
            if cnt:
                toks.append(_Tok(n, sem=e.sem, val=cnt))
        for c in self.chans.values():
            if c.count:
                toks.append(_Tok(c.queue, chan=c, val=c.count))
        for n in ENGINES:
            self.wait_tokens(n, toks)

    def barrier_token(self, ch):
        if isinstance(ch, str):
            ch = self.chans[ch]
        return _Tok(ch.queue, chan=ch, val=ch.count)

    def finalize(self):
        assert not self.finalized
        self.finalized = True
        block = self.nc.Block()
        block.__enter__()
        for name in ENGINES:
            e = self.eng[name]
            if not e.items:
                continue
            section = getattr(block, name)

            def emit(eng, _e=e):
                for fn, waits, sem, amt in _e.items:
                    for wsem, wval in waits:
                        eng.wait_ge(wsem, wval)
                    if fn is not None:
                        inst = fn(eng)
                        if sem is not None:
                            inst.then_inc(sem, amt)

            section(emit)
        block.__exit__(None, None, None)



def build(phases=3):
    nc = bass.Bass("TRN2", target_bir_lowering=False)

    x_full = nc.dram_tensor("x_full", [SS, EE], F32, kind="ExternalInput")
    x_own = nc.dram_tensor("x_own", [2048, EE], F32, kind="ExternalInput")
    wq_g = nc.dram_tensor("wq_g", [512, EE], F32, kind="ExternalInput")
    bq_g = nc.dram_tensor("bq_g", [1, 512], F32, kind="ExternalInput")
    wk_g = nc.dram_tensor("wk_g", [128, EE], F32, kind="ExternalInput")
    bk_g = nc.dram_tensor("bk_g", [1, 128], F32, kind="ExternalInput")
    wv_g = nc.dram_tensor("wv_g", [128, EE], F32, kind="ExternalInput")
    bv_g = nc.dram_tensor("bv_g", [1, 128], F32, kind="ExternalInput")
    wo_t = nc.dram_tensor("wo", [EE, EE], F32, kind="ExternalInput")
    bo_t = nc.dram_tensor("bo", [1, EE], F32, kind="ExternalInput")
    gate_pk = nc.dram_tensor("gate_pk", [GSH, 1024], I32, kind="ExternalInput")
    gate_sc = nc.dram_tensor("gate_sc", [GSH, 64], F32, kind="ExternalInput")
    up_pk = nc.dram_tensor("up_pk", [GSH, 1024], I32, kind="ExternalInput")
    up_sc = nc.dram_tensor("up_sc", [GSH, 64], F32, kind="ExternalInput")
    down_pk = nc.dram_tensor("down_pk", [DSH, 8192], I32, kind="ExternalInput")
    down_sc = nc.dram_tensor("down_sc", [DSH, 512], F32, kind="ExternalInput")
    identb_t = nc.dram_tensor("identb", [128, 128], BF16, kind="ExternalInput")
    identf_t = nc.dram_tensor("identf", [128, 128], F32, kind="ExternalInput")
    ropec_t = nc.dram_tensor("ropec", [64, SS], F32, kind="ExternalInput")
    ropes_t = nc.dram_tensor("ropes", [64, SS], F32, kind="ExternalInput")
    maskm_t = nc.dram_tensor("maskm", [128, 896], BF16, kind="ExternalInput")
    sel0_t = nc.dram_tensor("sel0", [128, 1], F32, kind="ExternalInput")
    sel1_t = nc.dram_tensor("sel1", [128, 1], F32, kind="ExternalInput")

    out_t = nc.dram_tensor("out", [2048, EE], F32, kind="ExternalOutput")

    ffn_shard = nc.dram_tensor("ffn_shard", [SH], BF16)
    ffn_ch = [nc.dram_tensor(f"ffn_ch{c}", [8 * CH], BF16) for c in range(4)]
    woT_scr = nc.dram_tensor("woT_scr", [16 * 16 * 128 * 128], BF16)
    qT_scr = nc.dram_tensor("qT_scr", [512, SS], BF16)
    xn_scr = nc.dram_tensor("xn_scr", [4, 128, ECH * BLK], BF16)
    a2a_in = [nc.dram_tensor(f"a2a_in{i}", [8, 256, 2048], BF16) for i in range(2)]
    a2a_out = [nc.dram_tensor(f"a2a_out{i}", [8, 256, 2048], BF16) for i in range(2)]

    P = Prog(nc)
    groups = defaultdict(list)
    alloc_order = []

    def sb(name, shape, dtype, grp):
        h = nc.sbuf_tensor(name, shape, dtype)
        t = h.__enter__()
        groups[grp].append(h)
        alloc_order.append(h)
        return t

    def free_group(grp):
        for h in reversed(groups.pop(grp, [])):
            h.__exit__(None, None, None)
            alloc_order.remove(h)

    def psum(name, shape, dtype):
        h = nc.psum_tensor(name, shape, dtype)
        t = h.__enter__()
        groups["psum"].append(h)
        return t

    identb = sb("identb_s", [128, 128], BF16, "persist")
    identf = sb("identf_s", [128, 128], F32, "persist")
    maskm = sb("maskm_s", [128, 896], BF16, "persist")
    onescol_b = sb("onescol_b", [128, 1], BF16, "persist")
    onescol_f = sb("onescol_f", [128, 1], F32, "persist")
    onesmat_f = sb("onesmat_f", [128, 128], F32, "persist")
    onesrow_b = sb("onesrow_b", [1, BLK], BF16, "persist")
    onesrow_f = sb("onesrow_f", [1, 128], F32, "persist")

    bq_sb = sb("bq_sb", [1, 512], BF16, "persist")
    bk_sb = sb("bk_sb", [1, 128], BF16, "persist")
    bv_sb = sb("bv_sb", [1, 128], BF16, "persist")
    bo_sb = sb("bo_sb", [1, EE], BF16, "persist")
    sel0 = sb("sel0_s", [128, 1], F32, "persist")
    sel1 = sb("sel1_s", [128, 1], F32, "persist")
    negB = sb("negB", [128, 1], F32, "persist")

    K_sb = sb("K_sb", [128, SS], BF16, "p12")
    V_sb = sb("V_sb", [128, SS], BF16, "p12")

    PP = [psum(f"pp{i}", [128, 2048], BF16) for i in range(2)]
    PPF = [p[:].bitcast(F32) for p in PP]
    PQ = [psum(f"pq{i}", [128, 1024], BF16) for i in range(4)]
    PQF = [p[:].bitcast(F32) for p in PQ]
    psA, psB = PQ[2], PQ[3]
    psM, psM_k = PPF[0][:, 0:512], "b0"
    psU, psU_k = PPF[0][:, 512:1024], "b1"

    P.chan("c_const", "sync", "bulk")
    P.chan("c_x", "sync", "bulk")
    P.chan("c_w", "sync", "bulk")
    P.chan("c_rope", "sync", "bulk")
    P.chan("c_spill", "gpsimd", "bulk")
    P.chan("c_q", "scalar", "bulk")
    P.chan("c_oT", "gpsimd", "bulk")
    P.chan("c_fw", "sync", "bulk")
    P.chan("c_fd", "sync", "bulk")
    P.chan("c_xo", "gpsimd", "bulk")
    P.chan("c_out", "gpsimd", "bulk")

    P.dma("c_const", out=identb[:], in_=identb_t[:])
    P.dma("c_const", out=identf[:], in_=identf_t[:])
    P.dma("c_const", out=maskm[:], in_=maskm_t[:])
    P.dma("c_const", out=sel0[:], in_=sel0_t[:])
    P.dma("c_const", out=sel1[:], in_=sel1_t[:])
    P.op("vector", lambda e: e.memset(onescol_b[:], 1.0), [], [onescol_b])
    P.op("vector", lambda e: e.memset(onescol_f[:], 1.0), [], [onescol_f])
    P.op("vector", lambda e: e.memset(onesmat_f[:], 1.0), [], [onesmat_f])
    P.op("vector", lambda e: e.memset(onesrow_b[:], 1.0), [], [onesrow_b])
    P.op("vector", lambda e: e.memset(onesrow_f[:], 1.0), [], [onesrow_f])
    for dst, srct in [(bq_sb, bq_g), (bk_sb, bk_g), (bv_sb, bv_g), (bo_sb, bo_t)]:
        P.dma("c_spill", out=dst[:], in_=srct[:])

    wqT = sb("wqT", [128, ECH * 512], BF16, "p1x")
    wkT = sb("wkT", [128, ECH * 128], BF16, "p1x")
    wvT = sb("wvT", [128, ECH * 128], BF16, "p1x")
    wbuf = [sb(f"wbuf{i}", [128, EE], F32, "p0") for i in range(2)]
    wbufb = sb("wbufb", [128, EE], BF16, "p0")
    pkb = [sb(f"pkb{i}", [128, 1024], I32, "p0") for i in range(2)]
    scb = [sb(f"scb{i}", [128, 64], F32, "p0") for i in range(2)]
    scbb = sb("scbb", [128, 64], BF16, "p0")
    dqw = [sb(f"dqw{i}", [128, 1024], I16, "p0") for i in range(1)]
    dqlo = [sb(f"dqlo{i}", [128, 1024], I16, "p0") for i in range(1)]
    dqhi = [sb(f"dqhi{i}", [128, 1024], I16, "p0") for i in range(1)]
    dqnib = [sb(f"dqnib{i}", [128, 2048], I16, "p0") for i in range(1)]
    dqm = sb("dqm", [128, 2048], I16, "p0")
    dqsgn = sb("dqsgn", [128, 2048], I16, "p0")
    dqge = sb("dqge", [128, 2048], I16, "p0")
    dqbits = sb("dqbits", [128, 2048], I16, "p0")
    dqval = sb("dqval", [128, 2048], BF16, "p0")
    asm = [sb(f"asm{i}", [128, 2048], BF16, "p0") for i in range(1)]

    def prep_wT(src, rows, dst):
        for t in range(rows // 128):
            wb = wbuf[t % 2]
            P.dma("c_w", out=wb[:], in_=src[t * 128:(t + 1) * 128, :], writes=[wb])
            P.op("vector", lambda e, wb=wb: e.tensor_copy(wbufb[:], wb[:]), [wb], [wbufb])
            for ec in range(ECH):
                ps = psA if ec % 2 == 0 else psB
                P.op("tensor", lambda e, ec=ec, ps=ps: e.transpose(ps[:, 0:128], wbufb[:, ec * 128:(ec + 1) * 128], identb[:]),
                     [wbufb, identb], [ps])
                c0 = ec * rows + t * 128
                P.op("vector", lambda e, ps=ps, c0=c0, dst=dst: e.tensor_copy(dst[:, c0:c0 + 128], ps[:, 0:128]), [ps], [dst])

    prep_wT(wq_g, 512, wqT)
    prep_wT(wk_g, 128, wkT)
    prep_wT(wv_g, 128, wvT)

    woT_v = woT_scr[:].rearrange("(oc p c) -> oc p c", oc=16, p=128)
    for t in range(16):
        wb = wbuf[t % 2]
        am = asm[0]
        P.dma("c_w", out=wb[:], in_=wo_t[t * 128:(t + 1) * 128, :], writes=[wb])
        P.op("vector", lambda e, wb=wb: e.tensor_copy(wbufb[:], wb[:]), [wb], [wbufb])
        for ic in range(16):
            ps = psA if ic % 2 == 0 else psB
            P.op("tensor", lambda e, ic=ic, ps=ps: e.transpose(ps[:, 0:128], wbufb[:, ic * 128:(ic + 1) * 128], identb[:]),
                 [wbufb, identb], [ps])
            P.op("vector", lambda e, ic=ic, ps=ps, am=am: e.tensor_copy(am[:, ic * 128:(ic + 1) * 128], ps[:, 0:128]), [ps], [am])
        P.dma("c_spill", out=woT_v[t], in_=am[:], reads=[am], writes=[f"woT#{t}"])

    def dequant_tile(pk_src, sc_src, rt, piece_dst):
        pb = pkb[rt % 2]
        scf = scb[rt % 2]
        P.dma("c_w", out=pb[:], in_=pk_src, writes=[pb])
        P.dma("c_w", out=scf[:], in_=sc_src, writes=[scf])
        dw, dl, dh, dn = dqw[0], dqlo[0], dqhi[0], dqnib[0]
        P.op("vector", lambda e, pb=pb, dw=dw: e.tensor_copy(dw[:], pb[:]), [pb], [dw])
        P.op("vector", lambda e, dw=dw, dl=dl: e.tensor_scalar(dl[:], dw[:], 15, None, ALU.bitwise_and), [dw], [dl])
        P.op("vector", lambda e, dw=dw, dh=dh: e.tensor_scalar(dh[:], dw[:], 4, 15, ALU.logical_shift_right, ALU.bitwise_and), [dw], [dh])
        nib_v = dn[:].rearrange("p (i two) -> p i two", two=2)
        P.op("vector", lambda e, nib_v=nib_v, dl=dl: e.tensor_copy(nib_v[:, :, 0], dl[:]), [dl], [dn])
        P.op("vector", lambda e, nib_v=nib_v, dh=dh: e.tensor_copy(nib_v[:, :, 1], dh[:]), [dh], [dn])
        P.op("vector", lambda e, dn=dn: e.tensor_scalar(dqm[:], dn[:], 7, None, ALU.bitwise_and), [dn], [dqm])
        P.op("vector", lambda e, dn=dn: e.tensor_scalar(dqsgn[:], dn[:], 8, 12, ALU.bitwise_and, ALU.logical_shift_left), [dn], [dqsgn])
        P.op("vector", lambda e: e.tensor_scalar(dqbits[:], dqm[:], 16128, None, ALU.mult), [dqm], [dqbits])
        P.op("vector", lambda e, dn=dn: e.tensor_scalar(dn[:], dqm[:], 64, 16128, ALU.mult, ALU.add), [dqm], [dn])
        P.op("vector", lambda e: e.tensor_scalar(dqge[:], dqm[:], 2, None, ALU.is_ge), [dqm], [dqge])
        P.op("vector", lambda e, dn=dn: e.copy_predicated(dqbits[:], dqge[:], dn[:]), [dqge, dn, dqbits], [dqbits])
        P.op("vector", lambda e: e.tensor_tensor(dqbits[:], dqbits[:], dqsgn[:], ALU.bitwise_or), [dqbits, dqsgn], [dqbits])
        P.op("vector", lambda e, scf=scf: e.tensor_copy(scbb[:], scf[:]), [scf], [scbb])
        sclb = scbb[:].unsqueeze(-1).broadcast_to([128, 64, 32])
        vv = dqbits[:].bitcast(BF16).rearrange("p (i b) -> p i b", b=32)
        ov = dqval[:].rearrange("p (i b) -> p i b", b=32)
        P.op("vector", lambda e, vv=vv, ov=ov, sclb=sclb: e.tensor_tensor(ov, vv, sclb, ALU.mult), [dqbits, scbb], [dqval])
        am = asm[0]
        for sub in range(16):
            ps = psA if sub % 2 == 0 else psB
            P.op("tensor", lambda e, sub=sub, ps=ps: e.transpose(ps[:, 0:128], dqval[:, sub * 128:(sub + 1) * 128], identb[:]),
                 [dqval, identb], [ps])
            P.op("vector", lambda e, sub=sub, ps=ps, am=am: e.tensor_copy(am[:, sub * 128:(sub + 1) * 128], ps[:, 0:128]), [ps], [am])
        dst, key = piece_dst(-1)
        src_ap = am[:] if dst.ndim == 2 else am[:].rearrange("p (ncl h) -> p ncl h", ncl=16)
        P.dma("c_spill", out=dst, in_=src_ap, reads=[am], writes=[key])

    deq_units = []
    for nt in range(16):
        for gu, pk_t, sc_t in [(0, gate_pk, gate_sc), (1, up_pk, up_sc)]:
            def gdst(sub, nt=nt, gu=gu):
                off = nt * 524288 + gu * 262144
                blk = ffn_shard[off: off + 262144]
                return (blk.rearrange("(p c) -> p c", p=128), f"ffs#{gu}#{nt}")

            deq_units.append(lambda nt=nt, gu=gu, pk_t=pk_t, sc_t=sc_t, gdst=gdst: dequant_tile(
                pk_t[nt * 128:(nt + 1) * 128, :], sc_t[nt * 128:(nt + 1) * 128, :], nt * 2 + gu, gdst))

    for cp in range(8):
        for ht in range(2):
            def ddst(sub, ht=ht, cp=cp):
                blk = ffn_shard[2 * SH_G + cp * 524288: 2 * SH_G + (cp + 1) * 524288]
                v = blk.rearrange("(p ncl h) -> p ncl h", p=128, ncl=16)
                return (v[:, :, ht * 128:(ht + 1) * 128], f"ffs#d#{cp}#{ht}")

            deq_units.append(lambda ht=ht, cp=cp, ddst=ddst: dequant_tile(
                down_pk[ht * 128:(ht + 1) * 128, cp * 1024:(cp + 1) * 1024],
                down_sc[ht * 128:(ht + 1) * 128, cp * 64:(cp + 1) * 64], cp * 2 + ht, ddst))

    chunk_keys = {0: [], 1: [], 2: [], 3: []}
    for nt in range(16):
        for gu in range(2):
            c = (nt * 524288 + gu * 262144) // CH
            chunk_keys[c].append(f"ffs#{gu}#{nt}")
    for cp in range(8):
        c = (2 * SH_G + cp * 524288) // CH
        chunk_keys[c].append(f"ffs#d#{cp}#0")
        chunk_keys[c].append(f"ffs#d#{cp}#1")
    AG_AT = {12: 0, 24: 1, 36: 2, 48: 3}

    deq_state = {"done": 0, "gp_snap": None}

    def emit_ag(c):
        if deq_state["gp_snap"] is None:
            deq_state["gp_snap"] = P.eng["gpsimd"].count
        P.wait_keys("gpsimd", chunk_keys[c])
        P.op("gpsimd", lambda e, c=c: e.collective_compute(
            "AllGather", ALU.bypass, replica_groups=[list(range(8))],
            ins=[ffn_shard[c * CH:(c + 1) * CH].opt()], outs=[ffn_ch[c][:].opt()]),
            chunk_keys[c], [f"ffnch{c}"])

    def pop_deq(n=1):
        for _ in range(n):
            if deq_units:
                deq_units.pop(0)()
                deq_state["done"] += 1
                ag = AG_AT.get(deq_state["done"])
                if ag is not None:
                    emit_ag(ag)

    if phases < 1:
        pop_deq(len(deq_units))
        P.barrier_all(limit={"gpsimd": deq_state["gp_snap"]})
        free_group("p0")

    if phases >= 1:
        xtile = [sb(f"xtile{i}", [128, EE], F32, "p1") for i in range(2)]
        lnscr = sb("lnscr", [128, EE // 2], F32, "p1")
        ln_s = [sb(f"ln_s{i}", [128, 1], F32, "p1") for i in range(2)]
        ln_ss = [sb(f"ln_ss{i}", [128, 1], F32, "p1") for i in range(2)]
        ss2 = sb("ss2", [128, 1], F32, "p1")
        ln_mu = [sb(f"ln_mu{i}", [128, 1], F32, "p1") for i in range(2)]
        ln_m2 = [sb(f"ln_m2{i}", [128, 1], F32, "p1") for i in range(2)]
        ln_r = [sb(f"ln_r{i}", [128, 1], F32, "p1") for i in range(2)]
        xnt_tok = [sb(f"xnt_tok{i}", [128, EE], BF16, "p1") for i in range(2)]
        xnT_b = [sb(f"xnT{i}", [128, ECH * BLK], BF16, "p1") for i in range(2)]
        vtmp = sb("vtmp", [128, BLK], BF16, "p1")
        ropec = [sb(f"ropecs{i}", [64, BLK], F32, "p1") for i in range(1)]
        ropes = [sb(f"ropess{i}", [64, BLK], F32, "p1") for i in range(1)]
        rp1 = [sb(f"rp1_{i}", [64, BLK], F32, "p1") for i in range(1)]
        rp2 = [sb(f"rp2_{i}", [64, BLK], F32, "p1") for i in range(1)]
        qspill = [sb(f"qspill{i}", [128, BLK], BF16, "p1") for i in range(4)]

        nmax = sb("nmax", [1, 1], F32, "p1")
        nmax_q = sb("nmax_q", [1, 1], F32, "p1")
        nmax_k = sb("nmax_k", [1, 1], F32, "p1")
        ntmp = sb("ntmp", [1, 1], F32, "p1")

        def ln_block(x_src_rows, ti):
            par = ti % 2
            xt = xtile[ti % 2]
            sc, s_, ss, mu, m2, r_ = lnscr, ln_s[par], ln_ss[par], ln_mu[par], ln_m2[par], ln_r[par]
            xo = xnt_tok[par]
            P.dma("c_x", out=xt[:], in_=x_src_rows, writes=[xt])
            P.op("vector", lambda e, xt=xt, s_=s_: e.tensor_reduce(s_[:], xt[:], AX.X, ALU.add), [xt], [s_])
            P.op("scalar", lambda e, xt=xt, sc=sc, ss=ss: e.activation(sc[:], xt[:, 0:EE // 2], AF.Square, accum_out=ss[:]), [xt], [sc, ss])
            P.op("scalar", lambda e, xt=xt, sc=sc, ss=ss: e.activation(sc[:], xt[:, EE // 2:EE], AF.Square, accum_out=ss2[:]), [xt], [sc, "ss2"])
            P.op("vector", lambda e, ss=ss: e.tensor_tensor(ss[:], ss[:], ss2[:], ALU.add), [ss, "ss2"], [ss])
            P.op("vector", lambda e, s_=s_, mu=mu: e.tensor_scalar(mu[:], s_[:], 1.0 / EE, None, ALU.mult), [s_], [mu])
            P.op("vector", lambda e, mu=mu, m2=m2: e.tensor_scalar(m2[:], mu[:], mu[:], 1e-5, ALU.mult, ALU.subtract), [mu], [m2])
            P.op("vector", lambda e, ss=ss, m2=m2: e.scalar_tensor_tensor(m2[:], ss[:], 1.0 / EE, m2[:], ALU.mult, ALU.subtract), [ss, m2], [m2])
            P.op("scalar", lambda e, m2=m2: e.activation(m2[:], m2[:], AF.Sqrt), [m2], [m2])
            P.op("vector", lambda e, m2=m2, r_=r_: e.reciprocal(r_[:], m2[:]), [m2], [r_])
            P.op("vector", lambda e, xt=xt, xo=xo, mu=mu, r_=r_: e.tensor_scalar(xo[:], xt[:], mu[:], r_[:], ALU.subtract, ALU.mult), [xt, mu, r_], [xo])
            return xo

        trring = [psA, psB, PQ[1]]

        def transpose_to(xsrc, dst, col_base, tsub):
            for ec in range(ECH):
                ps = trring[ec % 3]
                P.op("tensor", lambda e, ec=ec, ps=ps, xsrc=xsrc: e.transpose(ps[:, 0:128], xsrc[:, ec * 128:(ec + 1) * 128], identb[:]),
                     [xsrc, identb], [ps])
                c0 = col_base(ec) + tsub * 128
                P.op("vector", lambda e, ps=ps, c0=c0, dst=dst: e.tensor_copy(dst[:, c0:c0 + 128], ps[:, 0:128]), [ps], [dst])

        def rope_apply(ps_src, pskey, dst, dst_key, c0, c1, rc, rs, par):
            r1, r2 = rp1[0], rp2[0]
            P.op("vector", lambda e, r1=r1: e.tensor_tensor(r1[:], ps_src[0:64, :], rc[0:64, :], ALU.mult), [pskey, rc], [r1])
            P.op("vector", lambda e, r2=r2: e.tensor_tensor(r2[:], ps_src[64:128, :], rs[0:64, :], ALU.mult), [pskey, rs], [r2])
            P.op("vector", lambda e, r1=r1, r2=r2: e.tensor_tensor(dst[0:64, c0:c1], r1[:], r2[:], ALU.subtract), [r1, r2], [dst_key])
            P.op("vector", lambda e, r1=r1: e.tensor_tensor(r1[:], ps_src[64:128, :], rc[0:64, :], ALU.mult), [pskey, rc], [r1])
            P.op("vector", lambda e, r2=r2: e.tensor_tensor(r2[:], ps_src[0:64, :], rs[0:64, :], ALU.mult), [pskey, rs], [r2])
            P.op("vector", lambda e, r1=r1, r2=r2: e.tensor_tensor(dst[64:128, c0:c1], r1[:], r2[:], ALU.add), [r1, r2], [dst_key])

        nsqv = vtmp
        def norm_track(src_ap, src_keys, tgt):
            P.op("vector", lambda e: e.tensor_tensor(nsqv[:], src_ap, src_ap, ALU.mult), src_keys, [nsqv])
            P.op("tensor", lambda e: e.matmul(PQF[1][0:1, :], onescol_b[:], nsqv[:], start=True, stop=True), [onescol_b, nsqv], ["pq1"])
            P.op("vector", lambda e: e.tensor_reduce(nmax[:], PQF[1][0:1, :], AX.X, ALU.max), ["pq1"], [nmax])
            P.op("vector", lambda e, tgt=tgt: e.tensor_tensor(tgt[:], tgt[:], nmax[:], ALU.max), [nmax, tgt], [tgt])

        P.op("vector", lambda e: e.memset(nmax_q[:], 0.0), [], [nmax_q])
        P.op("vector", lambda e: e.memset(nmax_k[:], 0.0), [], [nmax_k])
        psV, psV_k = PPF[1][:, 512:1024], "b3"
        psQ2, psQ2_k = PPF[1][:, 0:512], "b2"
        for tb in range(NB):
            xc = xnT_b[tb % 2]
            rc = ropec[0]
            rs = ropes[0]
            P.dma("c_rope", out=rc[:], in_=ropec_t[:, tb * BLK:(tb + 1) * BLK], writes=[rc])
            P.dma("c_rope", out=rs[:], in_=ropes_t[:, tb * BLK:(tb + 1) * BLK], writes=[rs])
            for tsub in range(4):
                r0 = tb * BLK + tsub * 128
                xo = ln_block(x_full[r0:r0 + 128, :], tb * 4 + tsub)
                transpose_to(xo, xc, lambda ec: ec * BLK, tsub)
            fns = []
            for ec in range(ECH):
                fns.append(lambda e, ec=ec, xc=xc: e.matmul(
                    psV, wvT[:, ec * 128:(ec + 1) * 128], xc[:, ec * BLK:(ec + 1) * BLK],
                    start=(ec == 0), stop=False))
            fns.append(lambda e: e.matmul(psV, bv_sb[:], onesrow_b[:], start=False, stop=True))
            P.op_group("tensor", fns, [xc, wvT, bv_sb, onesrow_b], [psV_k])
            P.op("vector", lambda e: e.tensor_copy(vtmp[:], psV), [psV_k], [vtmp])
            for tsub in range(4):
                ps = trring[tsub % 3]
                P.op("tensor", lambda e, tsub=tsub, ps=ps: e.transpose(ps[:, 0:128], vtmp[:, tsub * 128:(tsub + 1) * 128], identb[:]),
                     [vtmp, identb], [ps])
                kb_i = tb * 4 + tsub
                P.op("vector", lambda e, kb_i=kb_i, ps=ps: e.tensor_copy(V_sb[:, kb_i * 128:(kb_i + 1) * 128], ps[:, 0:128]), [ps], [V_sb])
            fns = []
            for ec in range(ECH):
                fns.append(lambda e, ec=ec, xc=xc: e.matmul(
                    psM, wkT[:, ec * 128:(ec + 1) * 128], xc[:, ec * BLK:(ec + 1) * BLK],
                    start=(ec == 0), stop=False))
            fns.append(lambda e: e.matmul(psM, bk_sb[:], onesrow_b[:], start=False, stop=True))
            P.op_group("tensor", fns, [xc, wkT, bk_sb, onesrow_b], [psM_k])
            rope_apply(psM, psM_k, K_sb, K_sb, tb * BLK, (tb + 1) * BLK, rc, rs, 0)
            for hc in range(4):
                qps, qps_k = (psU, psU_k) if hc % 2 == 0 else (psQ2, psQ2_k)
                qs = qspill[hc]
                fns = []
                for ec in range(ECH):
                    fns.append(lambda e, ec=ec, hc=hc, qps=qps, xc=xc: e.matmul(
                        qps, wqT[:, ec * 512 + hc * 128: ec * 512 + (hc + 1) * 128],
                        xc[:, ec * BLK:(ec + 1) * BLK], start=(ec == 0), stop=False))
                fns.append(lambda e, hc=hc, qps=qps: e.matmul(qps, bq_sb[0:1, hc * 128:(hc + 1) * 128], onesrow_b[:], start=False, stop=True))
                P.op_group("tensor", fns, [xc, wqT, bq_sb, onesrow_b], [qps_k])
                rope_apply(qps, qps_k, qs, qs, 0, BLK, rc, rs, hc % 2)
                P.dma("c_spill", out=qT_scr[hc * 128:(hc + 1) * 128, tb * BLK:(tb + 1) * BLK], in_=qs[:],
                      reads=[qs], writes=[f"qT#{hc}#{tb}"])
            norm_track(K_sb[:, tb * BLK:(tb + 1) * BLK], [K_sb], nmax_k)
            for hc in range(4):
                norm_track(qspill[hc][:], [qspill[hc]], nmax_q)
            pop_deq(3)
        for tb in range(4):
            xc = xnT_b[tb % 2]
            for tsub in range(4):
                r0 = tb * BLK + tsub * 128
                xo = ln_block(x_own[r0:r0 + 128, :], tb * 4 + tsub)
                transpose_to(xo, xc, lambda ec: ec * BLK, tsub)
            P.dma("c_spill", out=xn_scr[tb], in_=xc[:], reads=[xc], writes=[f"xns#{tb}"])
            pop_deq(1)
        P.op("vector", lambda e: e.tensor_tensor(ntmp[:], nmax_q[:], nmax_k[:], ALU.mult), [nmax_q, nmax_k], [ntmp])
        P.op("scalar", lambda e: e.activation(ntmp[:], ntmp[:], AF.Sqrt, scale=SCALE * SCALE), [ntmp], [ntmp])
        P.op("vector", lambda e: e.tensor_scalar(ntmp[:], ntmp[:], -1.0, None, ALU.mult), [ntmp], [ntmp])
        P.op("tensor", lambda e: e.matmul(PQF[1][:, 0:1], onesrow_f[0:1, :], ntmp[:], start=True, stop=True), [onesrow_f, ntmp], ["pq1"])
        P.op("vector", lambda e: e.tensor_copy(negB[:], PQF[1][:, 0:1]), ["pq1"], [negB])

        pop_deq(len(deq_units))

        P.barrier_all(limit={"gpsimd": deq_state["gp_snap"]})
        free_group("p1")
        free_group("p0")
        free_group("p1x")

    if phases >= 2:
        qh_buf = [sb(f"qh{i}", [128, BLK], BF16, "p23") for i in range(2)]
        pt_buf = [sb(f"pt{i}", [128, 2 * BLK], BF16, "p23") for i in range(4)]
        pacc_b = [sb(f"pacc{i}", [128, BLK], F32, "p23") for i in range(2)]
        rb_sb = sb("rb_sb", [128, BLK], F32, "p23")
        oh_sb = sb("oh_sb", [128, BLK], BF16, "p23")
        oTa = [sb(f"oTa{i}", [128, BLK], BF16, "p23") for i in range(2)]
        oTb = [sb(f"oTb{i}", [128, BLK], BF16, "p23") for i in range(2)]
        oTt = sb("oTt", [128, BLK], F32, "p23")
        oT_qt = sb("oT_qt", [128, 16 * BLK], BF16, "p23")
        woT_buf = [sb(f"woTb{i}", [128, 16 * 128], BF16, "p23") for i in range(2)]
        wgu_buf = [sb(f"wgu{i}", [128, 32 * 128], BF16, "p23") for i in range(2)]
        wd_buf = [sb(f"wdb{i}", [128, 16 * 256], BF16, "p23") for i in range(2)]
        inter = [sb(f"inter{i}", [128, BLK], BF16, "p23") for i in range(16)]
        xn_ffn = sb("xn_ffn", [128, ECH * BLK], BF16, "p23")
        gelu_s = [sb(f"gelu_s{i}", [128, BLK], F32, "p23") for i in range(2)]
        acc = sb("acc", [128, 16 * BLK], F32, "p23")
        xo_tile_b = [sb(f"xo{i}", [128, EE], F32, "p23") for i in range(1)]
        out_sb_b = [sb(f"outsb{i}", [128, EE], F32, "p23") for i in range(1)]

        psSC = PPF[0]
        psG, psG_k = PPF[1][:, 0:512], "b2"
        psU2, psU2_k = PPF[1][:, 512:1024], "b3"

        def xn_load_unit(qt):
            def f():
                P.dma("c_fw", out=xn_ffn[:], in_=xn_scr[qt], reads=[f"xns#{qt}"], writes=[xn_ffn])
            return f

        def gu_unit(qt, grp, nch):
            def f():
                wgb = wgu_buf[nch % 2]
                off = nch * 524288
                c = off // CH
                loc = grp * CH + off - c * CH
                src = ffn_ch[c][loc: loc + 524288]
                P.dma("c_fw", out=wgb[:].rearrange("p (gu c) -> p gu c", gu=2),
                      in_=src.rearrange("(gu p c) -> p gu c", gu=2, p=128),
                      reads=[f"ffnch{c}"], writes=[wgb])
                fns = []
                for ec in range(ECH):
                    fns.append(lambda e, ec=ec, wgb=wgb: e.matmul(
                        psG, wgb[:, ec * 128:(ec + 1) * 128], xn_ffn[:, ec * BLK:(ec + 1) * BLK],
                        start=(ec == 0), stop=(ec == ECH - 1)))
                P.op_group("tensor", fns, [wgb, xn_ffn], [psG_k])
                fns = []
                for ec in range(ECH):
                    fns.append(lambda e, ec=ec, wgb=wgb: e.matmul(
                        psU2, wgb[:, 2048 + ec * 128: 2048 + (ec + 1) * 128], xn_ffn[:, ec * BLK:(ec + 1) * BLK],
                        start=(ec == 0), stop=(ec == ECH - 1)))
                P.op_group("tensor", fns, [wgb, xn_ffn], [psU2_k])
                gl = gelu_s[nch % 2]
                it = inter[nch]
                P.op("scalar", lambda e, gl=gl: e.activation(gl[:], psG, AF.Gelu_apprx_tanh), [psG_k], [gl])
                P.op("vector", lambda e, it=it, gl=gl: e.tensor_tensor(it[:], gl[:], psU2, ALU.mult), [gl, psU2_k], [it])
            return f

        def down_unit(qt, grp, sh):
            def f():
                wdb = wd_buf[sh % 2]
                off = 2 * SH_G + grp * 524288
                c = off // CH
                loc = sh * CH + off - c * CH
                src = ffn_ch[c][loc: loc + 524288]
                P.dma("c_fd", out=wdb[:], in_=src.rearrange("(p c) -> p c", p=128),
                      reads=[f"ffnch{c}"], writes=[wdb])
                for hcl in range(2):
                    hc_g = sh * 2 + hcl
                    ps, pskey = (psG, psG_k) if hcl == 0 else (psU2, psU2_k)
                    fns = []
                    for ncl in range(16):
                        fns.append(lambda e, ncl=ncl, hcl=hcl, ps=ps, wdb=wdb: e.matmul(
                            ps, wdb[:, ncl * 256 + hcl * 128: ncl * 256 + (hcl + 1) * 128],
                            inter[ncl][:], start=(ncl == 0), stop=(ncl == 15)))
                    P.op_group("tensor", fns, [wdb] + [inter[n] for n in range(16)], [pskey])
                    if grp == 0:
                        P.op("vector", lambda e, hc_g=hc_g, ps=ps: e.tensor_copy(
                            acc[:, hc_g * BLK:(hc_g + 1) * BLK], ps), [pskey], [f"acc#{hc_g}"])
                    else:
                        P.op("vector", lambda e, hc_g=hc_g, ps=ps: e.tensor_tensor(
                            acc[:, hc_g * BLK:(hc_g + 1) * BLK], acc[:, hc_g * BLK:(hc_g + 1) * BLK], ps, ALU.add),
                            [pskey, f"acc#{hc_g}"], [f"acc#{hc_g}"])
            return f

        ffn_units = []
        for qt in range(4):
            ffn_units.append(xn_load_unit(qt))
            for grp in range(8):
                for nch in range(16):
                    ffn_units.append(gu_unit(qt, grp, nch))
                for sh in range(8):
                    ffn_units.append(down_unit(qt, grp, sh))
        UPQ = 1 + 8 * (16 + 8)
        ffn_state = {"i": 0}

        def pop_ffn(n, cap):
            cap = min(cap, len(ffn_units))
            while n > 0 and ffn_state["i"] < cap:
                ffn_units[ffn_state["i"]]()
                ffn_state["i"] += 1
                n -= 1

        a2a_keys = [[], []]
        pending = []
        from concourse.aot_env import allow_ici_all_to_all

        def emit_a2a(half):
            for fn in pending:
                fn()
            pending.clear()
            P.wait_keys("gpsimd", a2a_keys[half])

            def do_a2a(e, half=half):
                with allow_ici_all_to_all():
                    return e.collective_compute(
                        "AllToAll", ALU.bypass, replica_groups=[list(range(8))],
                        ins=[a2a_in[half][:].opt()], outs=[a2a_out[half][:].opt()])

            P.op("gpsimd", do_a2a, a2a_keys[half], [a2a_out[half]])
        pair_ct = {"n": 0}
        for hc in range(4):
            for qt in range(NB):
                qb = qh_buf[qt % 2]
                if qt == 0:
                    P.dma("c_q", out=qb[:], in_=qT_scr[hc * 128:(hc + 1) * 128, qt * BLK:(qt + 1) * BLK],
                          reads=[f"qT#{hc}#{qt}"], writes=[qb])
                nkc = (qt + 1) * 4
                if qt + 1 < NB:
                    nqb = qh_buf[(qt + 1) % 2]
                    P.dma("c_q", out=nqb[:], in_=qT_scr[hc * 128:(hc + 1) * 128, (qt + 1) * BLK:(qt + 2) * BLK],
                          reads=[f"qT#{hc}#{qt + 1}"], writes=[nqb])
                elif hc + 1 < 4:
                    nqb = qh_buf[0]
                    P.dma("c_q", out=nqb[:], in_=qT_scr[(hc + 1) * 128:(hc + 2) * 128, 0:BLK],
                          reads=[f"qT#{hc + 1}#0"], writes=[nqb])
                psO, psO_k = (PQF[0], "pq0") if qt % 2 == 0 else (PQF[1], "pq1")
                psD, psD_k = (PQF[2], "pq2") if qt % 2 == 0 else (PQF[3], "pq3")
                pacc = pacc_b[qt % 2]

                def emit_score(kb, qb=qb, qt=qt, pacc=pacc):
                    pt = pt_buf[(kb // 2) % 4]
                    pr = (kb // 2) % 2
                    ps = PPF[pr]
                    kA, kB = ("b0", "b1") if pr == 0 else ("b2", "b3")
                    for h in range(2):
                        P.op("tensor", lambda e, kb=kb, h=h, qb=qb, ps=ps: e.matmul(
                            ps[:, h * 512:(h + 1) * 512], K_sb[:, (kb + h) * 128:(kb + h + 1) * 128], qb[:],
                            start=True, stop=True),
                            [K_sb, qb], [kA if h == 0 else kB])
                    P.op("scalar", lambda e, pt=pt, ps=ps: e.activation(
                        pt[:], ps[:], AF.Exp, bias=negB[:], scale=SCALE),
                        [kA, kB, negB], [pt])
                    for h in range(2):
                        d = (kb + h) * 128 - qt * BLK
                        if d >= 0:
                            off = 384 - d
                            P.op("vector", lambda e, pt=pt, off=off, h=h: e.tensor_tensor(
                                pt[:, h * 512:(h + 1) * 512], pt[:, h * 512:(h + 1) * 512], maskm[:, off:off + BLK], ALU.mult),
                                [pt, maskm], [pt])
                    if kb == 0:
                        P.op("vector", lambda e, pt=pt, pacc=pacc: e.tensor_copy(pacc[:], pt[:, 0:512]), [pt], [pacc])
                    else:
                        P.op("vector", lambda e, pt=pt, pacc=pacc: e.tensor_tensor(pacc[:], pacc[:], pt[:, 0:512], ALU.add), [pt, pacc], [pacc])
                    P.op("vector", lambda e, pt=pt, pacc=pacc: e.tensor_tensor(pacc[:], pacc[:], pt[:, 512:1024], ALU.add), [pt, pacc], [pacc])

                def emit_pv(kb, nkc=nkc, psO=psO, psO_k=psO_k):
                    pt = pt_buf[(kb // 2) % 4]
                    fns = []
                    for h in range(2):
                        fns.append(lambda e, kb=kb, h=h, pt=pt, nkc=nkc, psO=psO: e.matmul(
                            psO, V_sb[:, (kb + h) * 128:(kb + h + 1) * 128],
                            pt[:, h * 512:(h + 1) * 512], start=(kb + h == 0), stop=(kb + h == nkc - 1)))
                    P.op_group("tensor", fns, [V_sb, pt], [psO_k])

                emit_score(0)
                for kb in range(0, nkc, 2):
                    if kb + 2 < nkc:
                        emit_score(kb + 2)
                    emit_pv(kb)
                    pair_ct["n"] += 1
                    if pair_ct["n"] % 24 == 0:
                        pop_ffn(8, UPQ + 17)

                def division(hc=hc, qt=qt, psO=psO, psD=psD, psD_k=psD_k, psO_k=psO_k, pacc=pacc):
                    P.op("tensor", lambda e: e.matmul(psD[:], onesmat_f[:], pacc[:], start=True, stop=True),
                         [onesmat_f, pacc], [psD_k])
                    P.op("vector", lambda e: e.reciprocal(rb_sb[:], psD[:]), [psD_k], [rb_sb])
                    P.op("vector", lambda e, psO=psO: e.tensor_tensor(oh_sb[:], psO, rb_sb[:], ALU.mult), [psO_k, rb_sb], [oh_sb])
                    qu, qc = qt // 4, qt % 4
                    half, row = hc // 2, (hc % 2) * 128
                    for piece in (qu, qu + 4):
                        key = f"a2a#{piece}#{hc}#{qt}"
                        a2a_keys[half].append(key)
                        P.dma("c_spill", out=a2a_in[half][piece, row:row + 128, qc * BLK:(qc + 1) * BLK],
                              in_=oh_sb[:], reads=[oh_sb], writes=[key])

                pending.append(division)
                if len(pending) > 1:
                    pending.pop(0)()
            if hc == 1:
                emit_a2a(0)
        emit_a2a(1)

        woT_v2 = woT_scr[:].rearrange("(oc p c) -> oc p c", oc=16, p=128)

        def do_merge(qt):
            for hd in range(16):
                rr, sub = hd // 4, hd % 4
                half, row = sub // 2, (sub % 2) * 128
                ta = oTa[hd % 2]
                tb_ = oTb[hd % 2]
                P.dma("c_oT", out=ta[:], in_=a2a_out[half][rr, row:row + 128, qt * BLK:(qt + 1) * BLK],
                      reads=[a2a_out[half]], writes=[ta])
                P.dma("c_oT", out=tb_[:], in_=a2a_out[half][rr + 4, row:row + 128, qt * BLK:(qt + 1) * BLK],
                      reads=[a2a_out[half]], writes=[tb_])
                P.op("vector", lambda e, ta=ta: e.tensor_scalar(oTt[:], ta[:], sel0[:], None, ALU.mult), [ta, sel0], [oTt])
                P.op("vector", lambda e, tb_=tb_, hd=hd: e.scalar_tensor_tensor(
                    oT_qt[:, hd * BLK:(hd + 1) * BLK], tb_[:], sel1[:], oTt[:], ALU.mult, ALU.add),
                    [tb_, sel1, oTt], [oT_qt])

        for qt in range(4):
            pop_ffn((qt + 1) * UPQ - ffn_state["i"], (qt + 1) * UPQ)
            do_merge(qt)
            pop_ffn(8, (qt + 1) * UPQ + 17)
            for oc in range(16):
                wob = woT_buf[oc % 2]
                P.dma("c_oT", out=wob[:], in_=woT_v2[oc], reads=[f"woT#{oc}"], writes=[wob])
                ps, pskey = (PQF[0], "pq0") if oc % 2 == 0 else (PQF[1], "pq1")
                fns = []
                for hd in range(16):
                    fns.append(lambda e, hd=hd, ps=ps, wob=wob: e.matmul(
                        ps, wob[:, hd * 128:(hd + 1) * 128], oT_qt[:, hd * BLK:(hd + 1) * BLK],
                        start=(hd == 0), stop=False))
                fns.append(lambda e, oc=oc, ps=ps: e.matmul(
                    ps, bo_sb[0:1, oc * 128:(oc + 1) * 128], onesrow_b[:], start=False, stop=True))
                P.op_group("tensor", fns, [wob, oT_qt, bo_sb, onesrow_b], [pskey])
                P.op("vector", lambda e, oc=oc, ps=ps: e.tensor_tensor(
                    acc[:, oc * BLK:(oc + 1) * BLK], acc[:, oc * BLK:(oc + 1) * BLK], ps, ALU.add),
                    [pskey, f"acc#{oc}"], [f"acc#{oc}"])
            pop_ffn((qt + 1) * UPQ + 17 - ffn_state["i"], (qt + 1) * UPQ + 17)
            for tsub in range(4):
                r0 = qt * BLK + tsub * 128
                xo_tile = xo_tile_b[0]
                out_sb = out_sb_b[0]
                P.dma("c_xo", out=xo_tile[:], in_=x_own[r0:r0 + 128, :], writes=[xo_tile])
                for oc in range(16):
                    trp, trk = (PQF[2], "pq2") if oc % 2 == 0 else (PQF[3], "pq3")
                    P.op("tensor", lambda e, oc=oc, tsub=tsub, trp=trp: e.transpose(
                        trp[:, 0:128], acc[:, oc * BLK + tsub * 128: oc * BLK + (tsub + 1) * 128], identf[:]),
                        [f"acc#{oc}", identf], [trk], pe_relaxed=False)
                    P.op("vector", lambda e, oc=oc, trp=trp, out_sb=out_sb, xo_tile=xo_tile: e.tensor_tensor(
                        out_sb[:, oc * 128:(oc + 1) * 128], trp[:, 0:128], xo_tile[:, oc * 128:(oc + 1) * 128], ALU.add),
                        [trk, xo_tile], [out_sb])
                P.dma("c_out", out=out_t[r0:r0 + 128, :], in_=out_sb[:], reads=[out_sb], writes=[f"out#{qt}#{tsub}"])

        P.wait_tokens("gpsimd", [P.barrier_token("c_out")])

    P.finalize()
    for h in reversed(alloc_order):
        h.__exit__(None, None, None)
    for h in reversed(groups.pop("psum", [])):
        h.__exit__(None, None, None)
    return nc



def _host_constants():
    identb = np.eye(128, dtype=np.float32).astype(ml_dtypes.bfloat16)
    identf = np.eye(128, dtype=np.float32)
    inv_freq = (1.0 / 10000.0 ** (np.arange(0, DH, 2, dtype=np.float32) / DH)).astype(np.float32)
    t = np.arange(SS, dtype=np.float32)
    ang = np.outer(inv_freq, t)
    ropec = np.cos(ang).astype(np.float32)
    ropes = np.sin(ang).astype(np.float32)
    i = np.arange(128)[:, None]
    u = np.arange(896)[None, :]
    maskm = ((u - i) >= 384).astype(np.float32).astype(ml_dtypes.bfloat16)
    return identb, identf, ropec, ropes, maskm


_CACHE = {}


def kernel(L0_attn_x, L0_attn_wq, L0_attn_bq, L0_attn_wk, L0_attn_bk,
           L0_attn_wv, L0_attn_bv, L0_attn_wo, L0_attn_bo,
           L0_ffn_gate_W_packed, L0_ffn_gate_scales,
           L0_ffn_up_W_packed, L0_ffn_up_scales,
           L0_ffn_down_W_packed, L0_ffn_down_scales,
           _phases=3):
    x = np.ascontiguousarray(np.asarray(L0_attn_x, dtype=np.float32))
    wq = np.asarray(L0_attn_wq, dtype=np.float32)
    bq = np.asarray(L0_attn_bq, dtype=np.float32)
    wk = np.asarray(L0_attn_wk, dtype=np.float32)
    bk = np.asarray(L0_attn_bk, dtype=np.float32)
    wv = np.asarray(L0_attn_wv, dtype=np.float32)
    bv = np.asarray(L0_attn_bv, dtype=np.float32)
    wo = np.asarray(L0_attn_wo, dtype=np.float32)
    bo = np.asarray(L0_attn_bo, dtype=np.float32)
    gpk = np.asarray(L0_ffn_gate_W_packed, dtype=np.int32)
    gsc = np.asarray(L0_ffn_gate_scales, dtype=np.float32)
    upk = np.asarray(L0_ffn_up_W_packed, dtype=np.int32)
    usc = np.asarray(L0_ffn_up_scales, dtype=np.float32)
    dpk = np.asarray(L0_ffn_down_W_packed, dtype=np.int32)
    dsc = np.asarray(L0_ffn_down_scales, dtype=np.float32)

    identb, identf, ropec, ropes, maskm = _host_constants()

    key = (_phases,)
    if key not in _CACHE:
        _CACHE[key] = build(phases=_phases)
    nc = _CACHE[key]

    in_maps = []
    for c in range(8):
        b, g = c // 4, c % 4
        sel0 = np.full((128, 1), 1.0 if b == 0 else 0.0, np.float32)
        in_maps.append(dict(
            x_full=x[b], x_own=np.ascontiguousarray(x[b, g * 2048:(g + 1) * 2048]),
            wq_g=np.ascontiguousarray(wq[g * 512:(g + 1) * 512]),
            bq_g=np.ascontiguousarray(bq[g * 512:(g + 1) * 512]).reshape(1, 512),
            wk_g=np.ascontiguousarray(wk[g * 128:(g + 1) * 128]),
            bk_g=np.ascontiguousarray(bk[g * 128:(g + 1) * 128]).reshape(1, 128),
            wv_g=np.ascontiguousarray(wv[g * 128:(g + 1) * 128]),
            bv_g=np.ascontiguousarray(bv[g * 128:(g + 1) * 128]).reshape(1, 128),
            wo=wo, bo=bo.reshape(1, EE),
            gate_pk=np.ascontiguousarray(gpk[c * GSH:(c + 1) * GSH]),
            gate_sc=np.ascontiguousarray(gsc[c * GSH:(c + 1) * GSH]),
            up_pk=np.ascontiguousarray(upk[c * GSH:(c + 1) * GSH]),
            up_sc=np.ascontiguousarray(usc[c * GSH:(c + 1) * GSH]),
            down_pk=np.ascontiguousarray(dpk[c * DSH:(c + 1) * DSH]),
            down_sc=np.ascontiguousarray(dsc[c * DSH:(c + 1) * DSH]),
            identb=identb, identf=identf, ropec=ropec, ropes=ropes,
            maskm=maskm, sel0=sel0, sel1=(1.0 - sel0).astype(np.float32),
        ))

    res = run_bass_kernel_spmd(nc, in_maps, core_ids=list(range(8)))
    out = np.empty((BB, SS, EE), np.float32)
    for c in range(8):
        b, g = c // 4, c % 4
        out[b, g * 2048:(g + 1) * 2048] = res.results[c]["out"]
    return out


# version 50 (final)
# speedup vs baseline: 1.0782x; 1.0782x over previous
import numpy as np
import ml_dtypes
from collections import defaultdict

import concourse.bass as bass
import concourse.mybir as mybir
from concourse.bass_utils import run_bass_kernel_spmd

F32 = mybir.dt.float32
BF16 = mybir.dt.bfloat16
I32 = mybir.dt.int32
I16 = mybir.dt.int16
AF = mybir.ActivationFunctionType
ALU = mybir.AluOpType
AX = mybir.AxisListType

BB, SS, EE = 2, 8192, 2048
H, HKV, DH = 16, 4, 128
NMLP = 16384
BLK = 512
NB = SS // BLK
ECH = EE // 128
SCALE = DH ** -0.5
GSH = 2048
DSH = 256
SH_G = 2048 * 2048
SH_D = 16384 * 256
SH = SH_G * 2 + SH_D
CH = SH // 4

ENGINES = ("sync", "scalar", "vector", "tensor", "gpsimd")


def _key_of(x):
    if isinstance(x, str):
        return x
    t = getattr(x, "tensor", None)
    if t is not None:
        return t.name
    return x.name


class _Chan:
    __slots__ = ("name", "queue", "sem", "count", "mode", "key")

    def __init__(self, nc, name, queue, mode):
        self.name = name
        self.queue = queue
        self.sem = nc.alloc_semaphore(name=f"ch_{name}")
        self.count = 0
        self.mode = mode
        self.key = None


class _Tok:
    __slots__ = ("eng", "sem", "val", "chan")

    def __init__(self, eng, sem=None, val=0, chan=None):
        self.eng = eng
        self.sem = sem
        self.val = val
        self.chan = chan

    def resolve(self):
        if self.chan is not None:
            if self.chan.mode == "bulk":
                return self.chan.sem, 16 * self.chan.count
            return self.chan.sem, 16 * self.val
        return self.sem, self.val


class _Eng:
    def __init__(self, nc, name):
        self.name = name
        self.sem = nc.alloc_semaphore(name=f"sem_{name}")
        self.count = 0
        self.waited = defaultdict(int)
        self.items = []


class Prog:
    def __init__(self, nc):
        self.nc = nc
        self.eng = {n: _Eng(nc, n) for n in ENGINES}
        self.chans = {}
        self.last_w = {}
        self.readers = defaultdict(list)
        self.finalized = False

    def chan(self, name, queue, mode="ring"):
        c = _Chan(self.nc, name, queue, mode)
        self.chans[name] = c
        return c

    def _wait(self, e, tok, waits):
        sem, val = tok.resolve()
        if e.waited[id(sem)] < val:
            waits.append((sem, val))
            e.waited[id(sem)] = val

    def _deps(self, engname, e, rkeys, wkeys, pe_relaxed):
        waits = []
        for k in rkeys:
            tok = self.last_w.get(k)
            if tok is not None and not (pe_relaxed and tok.eng == engname == "tensor"):
                self._wait(e, tok, waits)
        for k in wkeys:
            tok = self.last_w.get(k)
            if tok is not None and not (pe_relaxed and tok.eng == engname == "tensor"):
                self._wait(e, tok, waits)
            for rtok in self.readers.get(k, ()):
                if pe_relaxed and rtok.eng == engname == "tensor":
                    continue
                self._wait(e, rtok, waits)
        return waits

    def _record(self, tok, rkeys, wkeys):
        for k in wkeys:
            self.last_w[k] = tok
            self.readers[k] = []
        for k in rkeys:
            self.readers[k].append(tok)

    def op(self, engname, fn, reads=None, writes=None, pe_relaxed=True):
        return self.op_group(engname, [fn], reads, writes, pe_relaxed)

    def op_group(self, engname, fns, reads=None, writes=None, pe_relaxed=True):
        assert not self.finalized
        e = self.eng[engname]
        rkeys = [_key_of(r) for r in (reads or [])]
        wkeys = [_key_of(w) for w in (writes or [])]
        waits = self._deps(engname, e, rkeys, wkeys, pe_relaxed)
        e.count += 1
        tok = _Tok(engname, sem=e.sem, val=e.count)
        self._record(tok, rkeys, wkeys)
        n = len(fns)
        for i, fn in enumerate(fns):
            if i == n - 1:
                e.items.append((fn, waits if i == 0 else (), e.sem, 1))
            else:
                e.items.append((fn, waits if i == 0 else (), None, 0))
        return tok

    def dma(self, ch, out, in_, reads=None, writes=None, **kw):
        assert not self.finalized
        if isinstance(ch, str):
            ch = self.chans[ch]
        e = self.eng[ch.queue]
        rkeys = [_key_of(r) for r in (reads if reads is not None else [in_])]
        wkeys = [_key_of(w) for w in (writes if writes is not None else [out])]
        if ch.mode == "ring":
            k = tuple(sorted(wkeys))
            if ch.key is None:
                ch.key = k
            else:
                assert ch.key == k, f"ring chan {ch.name}: {k} != {ch.key}"
        waits = self._deps(ch.queue, e, rkeys, wkeys, pe_relaxed=False)
        ch.count += 1
        tok = _Tok(ch.queue, chan=ch, val=ch.count)
        self._record(tok, rkeys, wkeys)

        def fn(eng, _o=out, _i=in_, _kw=kw):
            return eng.dma_start(out=_o, in_=_i, **_kw)

        e.items.append((fn, waits, ch.sem, 16))
        return tok

    def wait_tokens(self, engname, tokens):
        e = self.eng[engname]
        waits = []
        for tok in tokens:
            self._wait(e, tok, waits)
        if waits:
            e.items.append((None, waits, None, 0))

    def wait_keys(self, engname, keys):
        toks = [self.last_w[_key_of(k)] for k in keys if _key_of(k) in self.last_w]
        self.wait_tokens(engname, toks)

    def barrier_all(self, limit=None):
        toks = []
        for n in ENGINES:
            e = self.eng[n]
            cnt = min(e.count, limit.get(n, e.count)) if limit else e.count
            if cnt:
                toks.append(_Tok(n, sem=e.sem, val=cnt))
        for c in self.chans.values():
            if c.count:
                toks.append(_Tok(c.queue, chan=c, val=c.count))
        for n in ENGINES:
            self.wait_tokens(n, toks)

    def barrier_token(self, ch):
        if isinstance(ch, str):
            ch = self.chans[ch]
        return _Tok(ch.queue, chan=ch, val=ch.count)

    def finalize(self):
        assert not self.finalized
        self.finalized = True
        block = self.nc.Block()
        block.__enter__()
        for name in ENGINES:
            e = self.eng[name]
            if not e.items:
                continue
            section = getattr(block, name)

            def emit(eng, _e=e):
                for fn, waits, sem, amt in _e.items:
                    for wsem, wval in waits:
                        eng.wait_ge(wsem, wval)
                    if fn is not None:
                        inst = fn(eng)
                        if sem is not None:
                            inst.then_inc(sem, amt)

            section(emit)
        block.__exit__(None, None, None)



def build(phases=3):
    nc = bass.Bass("TRN2", target_bir_lowering=False)

    x_full = nc.dram_tensor("x_full", [SS, EE], F32, kind="ExternalInput")
    x_own = nc.dram_tensor("x_own", [2048, EE], F32, kind="ExternalInput")
    wq_g = nc.dram_tensor("wq_g", [512, EE], F32, kind="ExternalInput")
    bq_g = nc.dram_tensor("bq_g", [1, 512], F32, kind="ExternalInput")
    wk_g = nc.dram_tensor("wk_g", [128, EE], F32, kind="ExternalInput")
    bk_g = nc.dram_tensor("bk_g", [1, 128], F32, kind="ExternalInput")
    wv_g = nc.dram_tensor("wv_g", [128, EE], F32, kind="ExternalInput")
    bv_g = nc.dram_tensor("bv_g", [1, 128], F32, kind="ExternalInput")
    wo_t = nc.dram_tensor("wo", [EE, EE], F32, kind="ExternalInput")
    bo_t = nc.dram_tensor("bo", [1, EE], F32, kind="ExternalInput")
    gate_pk = nc.dram_tensor("gate_pk", [GSH, 1024], I32, kind="ExternalInput")
    gate_sc = nc.dram_tensor("gate_sc", [GSH, 64], F32, kind="ExternalInput")
    up_pk = nc.dram_tensor("up_pk", [GSH, 1024], I32, kind="ExternalInput")
    up_sc = nc.dram_tensor("up_sc", [GSH, 64], F32, kind="ExternalInput")
    down_pk = nc.dram_tensor("down_pk", [DSH, 8192], I32, kind="ExternalInput")
    down_sc = nc.dram_tensor("down_sc", [DSH, 512], F32, kind="ExternalInput")
    identb_t = nc.dram_tensor("identb", [128, 128], BF16, kind="ExternalInput")
    identf_t = nc.dram_tensor("identf", [128, 128], F32, kind="ExternalInput")
    ropec_t = nc.dram_tensor("ropec", [64, SS], F32, kind="ExternalInput")
    ropes_t = nc.dram_tensor("ropes", [64, SS], F32, kind="ExternalInput")
    maskm_t = nc.dram_tensor("maskm", [128, 896], BF16, kind="ExternalInput")
    sel0_t = nc.dram_tensor("sel0", [128, 1], F32, kind="ExternalInput")
    sel1_t = nc.dram_tensor("sel1", [128, 1], F32, kind="ExternalInput")

    out_t = nc.dram_tensor("out", [2048, EE], F32, kind="ExternalOutput")

    ffn_shard = nc.dram_tensor("ffn_shard", [SH], BF16)
    ffn_ch = [nc.dram_tensor(f"ffn_ch{c}", [8 * CH], BF16) for c in range(4)]
    woT_scr = nc.dram_tensor("woT_scr", [16 * 16 * 128 * 128], BF16)
    qT_scr = nc.dram_tensor("qT_scr", [512, SS], BF16)
    xn_scr = nc.dram_tensor("xn_scr", [4, 128, ECH * BLK], BF16)
    a2a_in = [nc.dram_tensor(f"a2a_in{i}", [8, 256, 2048], BF16) for i in range(2)]
    a2a_out = [nc.dram_tensor(f"a2a_out{i}", [8, 256, 2048], BF16) for i in range(2)]

    P = Prog(nc)
    groups = defaultdict(list)
    alloc_order = []

    def sb(name, shape, dtype, grp):
        h = nc.sbuf_tensor(name, shape, dtype)
        t = h.__enter__()
        groups[grp].append(h)
        alloc_order.append(h)
        return t

    def free_group(grp):
        for h in reversed(groups.pop(grp, [])):
            h.__exit__(None, None, None)
            alloc_order.remove(h)

    def psum(name, shape, dtype):
        h = nc.psum_tensor(name, shape, dtype)
        t = h.__enter__()
        groups["psum"].append(h)
        return t

    identb = sb("identb_s", [128, 128], BF16, "persist")
    identf = sb("identf_s", [128, 128], F32, "persist")
    maskm = sb("maskm_s", [128, 896], BF16, "persist")
    onescol_b = sb("onescol_b", [128, 1], BF16, "persist")
    onescol_f = sb("onescol_f", [128, 1], F32, "persist")
    onesmat_f = sb("onesmat_f", [128, 128], F32, "persist")
    onesrow_b = sb("onesrow_b", [1, BLK], BF16, "persist")
    onesrow_f = sb("onesrow_f", [1, 128], F32, "persist")

    bq_sb = sb("bq_sb", [1, 512], BF16, "persist")
    bk_sb = sb("bk_sb", [1, 128], BF16, "persist")
    bv_sb = sb("bv_sb", [1, 128], BF16, "persist")
    bo_sb = sb("bo_sb", [1, EE], BF16, "persist")
    sel0 = sb("sel0_s", [128, 1], F32, "persist")
    sel1 = sb("sel1_s", [128, 1], F32, "persist")
    negB = sb("negB", [128, 1], F32, "persist")

    K_sb = sb("K_sb", [128, SS], BF16, "p12")
    V_sb = sb("V_sb", [128, SS], BF16, "p12")

    PP = [psum(f"pp{i}", [128, 2048], BF16) for i in range(2)]
    PPF = [p[:].bitcast(F32) for p in PP]
    PQ = [psum(f"pq{i}", [128, 1024], BF16) for i in range(4)]
    PQF = [p[:].bitcast(F32) for p in PQ]
    psA, psB = PQ[2], PQ[3]
    psM, psM_k = PPF[0][:, 0:512], "b0"
    psU, psU_k = PPF[0][:, 512:1024], "b1"

    P.chan("c_const", "sync", "bulk")
    P.chan("c_x", "sync", "bulk")
    P.chan("c_w", "sync", "bulk")
    P.chan("c_rope", "sync", "bulk")
    P.chan("c_spill", "gpsimd", "bulk")
    P.chan("c_q", "scalar", "bulk")
    P.chan("c_oT", "gpsimd", "bulk")
    P.chan("c_fw", "sync", "bulk")
    P.chan("c_fd", "sync", "bulk")
    P.chan("c_xo", "gpsimd", "bulk")
    P.chan("c_out", "gpsimd", "bulk")

    P.dma("c_const", out=identb[:], in_=identb_t[:])
    P.dma("c_const", out=identf[:], in_=identf_t[:])
    P.dma("c_const", out=maskm[:], in_=maskm_t[:])
    P.dma("c_const", out=sel0[:], in_=sel0_t[:])
    P.dma("c_const", out=sel1[:], in_=sel1_t[:])
    P.op("vector", lambda e: e.memset(onescol_b[:], 1.0), [], [onescol_b])
    P.op("vector", lambda e: e.memset(onescol_f[:], 1.0), [], [onescol_f])
    P.op("vector", lambda e: e.memset(onesmat_f[:], 1.0), [], [onesmat_f])
    P.op("vector", lambda e: e.memset(onesrow_b[:], 1.0), [], [onesrow_b])
    P.op("vector", lambda e: e.memset(onesrow_f[:], 1.0), [], [onesrow_f])
    for dst, srct in [(bq_sb, bq_g), (bk_sb, bk_g), (bv_sb, bv_g), (bo_sb, bo_t)]:
        P.dma("c_spill", out=dst[:], in_=srct[:])

    wqT = sb("wqT", [128, ECH * 512], BF16, "p1x")
    wkT = sb("wkT", [128, ECH * 128], BF16, "p1x")
    wvT = sb("wvT", [128, ECH * 128], BF16, "p1x")
    wbuf = [sb(f"wbuf{i}", [128, EE], F32, "p0") for i in range(2)]
    wbufb = sb("wbufb", [128, EE], BF16, "p0")
    pkb = [sb(f"pkb{i}", [128, 1024], I32, "p0") for i in range(2)]
    scb = [sb(f"scb{i}", [128, 64], F32, "p0") for i in range(2)]
    scbb = sb("scbb", [128, 64], BF16, "p0")
    dqw = [sb(f"dqw{i}", [128, 1024], I16, "p0") for i in range(1)]
    dqlo = [sb(f"dqlo{i}", [128, 1024], I16, "p0") for i in range(1)]
    dqhi = [sb(f"dqhi{i}", [128, 1024], I16, "p0") for i in range(1)]
    dqnib = [sb(f"dqnib{i}", [128, 2048], I16, "p0") for i in range(1)]
    dqm = sb("dqm", [128, 2048], I16, "p0")
    dqsgn = sb("dqsgn", [128, 2048], I16, "p0")
    dqge = sb("dqge", [128, 2048], I16, "p0")
    dqbits = sb("dqbits", [128, 2048], I16, "p0")
    dqval = sb("dqval", [128, 2048], BF16, "p0")
    asm = [sb(f"asm{i}", [128, 2048], BF16, "p0") for i in range(1)]

    def prep_wT(src, rows, dst):
        for t in range(rows // 128):
            wb = wbuf[t % 2]
            P.dma("c_w", out=wb[:], in_=src[t * 128:(t + 1) * 128, :], writes=[wb])
            P.op("vector", lambda e, wb=wb: e.tensor_copy(wbufb[:], wb[:]), [wb], [wbufb])
            for ec in range(ECH):
                ps = psA if ec % 2 == 0 else psB
                P.op("tensor", lambda e, ec=ec, ps=ps: e.transpose(ps[:, 0:128], wbufb[:, ec * 128:(ec + 1) * 128], identb[:]),
                     [wbufb, identb], [ps])
                c0 = ec * rows + t * 128
                P.op("vector", lambda e, ps=ps, c0=c0, dst=dst: e.tensor_copy(dst[:, c0:c0 + 128], ps[:, 0:128]), [ps], [dst])

    prep_wT(wq_g, 512, wqT)
    prep_wT(wk_g, 128, wkT)
    prep_wT(wv_g, 128, wvT)

    woT_v = woT_scr[:].rearrange("(oc p c) -> oc p c", oc=16, p=128)
    for t in range(16):
        wb = wbuf[t % 2]
        am = asm[0]
        P.dma("c_w", out=wb[:], in_=wo_t[t * 128:(t + 1) * 128, :], writes=[wb])
        P.op("vector", lambda e, wb=wb: e.tensor_copy(wbufb[:], wb[:]), [wb], [wbufb])
        for ic in range(16):
            ps = psA if ic % 2 == 0 else psB
            P.op("tensor", lambda e, ic=ic, ps=ps: e.transpose(ps[:, 0:128], wbufb[:, ic * 128:(ic + 1) * 128], identb[:]),
                 [wbufb, identb], [ps])
            P.op("vector", lambda e, ic=ic, ps=ps, am=am: e.tensor_copy(am[:, ic * 128:(ic + 1) * 128], ps[:, 0:128]), [ps], [am])
        P.dma("c_spill", out=woT_v[t], in_=am[:], reads=[am], writes=[f"woT#{t}"])

    def dequant_tile(pk_src, sc_src, rt, piece_dst):
        pb = pkb[rt % 2]
        scf = scb[rt % 2]
        P.dma("c_w", out=pb[:], in_=pk_src, writes=[pb])
        P.dma("c_w", out=scf[:], in_=sc_src, writes=[scf])
        dw, dl, dh, dn = dqw[0], dqlo[0], dqhi[0], dqnib[0]
        P.op("vector", lambda e, pb=pb, dw=dw: e.tensor_copy(dw[:], pb[:]), [pb], [dw])
        P.op("vector", lambda e, dw=dw, dl=dl: e.tensor_scalar(dl[:], dw[:], 15, None, ALU.bitwise_and), [dw], [dl])
        P.op("vector", lambda e, dw=dw, dh=dh: e.tensor_scalar(dh[:], dw[:], 4, 15, ALU.logical_shift_right, ALU.bitwise_and), [dw], [dh])
        nib_v = dn[:].rearrange("p (i two) -> p i two", two=2)
        P.op("vector", lambda e, nib_v=nib_v, dl=dl: e.tensor_copy(nib_v[:, :, 0], dl[:]), [dl], [dn])
        P.op("vector", lambda e, nib_v=nib_v, dh=dh: e.tensor_copy(nib_v[:, :, 1], dh[:]), [dh], [dn])
        P.op("vector", lambda e, dn=dn: e.tensor_scalar(dqm[:], dn[:], 7, None, ALU.bitwise_and), [dn], [dqm])
        P.op("vector", lambda e, dn=dn: e.tensor_scalar(dqsgn[:], dn[:], 8, 12, ALU.bitwise_and, ALU.logical_shift_left), [dn], [dqsgn])
        P.op("vector", lambda e: e.tensor_scalar(dqbits[:], dqm[:], 16128, None, ALU.mult), [dqm], [dqbits])
        P.op("vector", lambda e, dn=dn: e.tensor_scalar(dn[:], dqm[:], 64, 16128, ALU.mult, ALU.add), [dqm], [dn])
        P.op("vector", lambda e: e.tensor_scalar(dqge[:], dqm[:], 2, None, ALU.is_ge), [dqm], [dqge])
        P.op("vector", lambda e, dn=dn: e.copy_predicated(dqbits[:], dqge[:], dn[:]), [dqge, dn, dqbits], [dqbits])
        P.op("vector", lambda e: e.tensor_tensor(dqbits[:], dqbits[:], dqsgn[:], ALU.bitwise_or), [dqbits, dqsgn], [dqbits])
        P.op("vector", lambda e, scf=scf: e.tensor_copy(scbb[:], scf[:]), [scf], [scbb])
        sclb = scbb[:].unsqueeze(-1).broadcast_to([128, 64, 32])
        vv = dqbits[:].bitcast(BF16).rearrange("p (i b) -> p i b", b=32)
        ov = dqval[:].rearrange("p (i b) -> p i b", b=32)
        P.op("vector", lambda e, vv=vv, ov=ov, sclb=sclb: e.tensor_tensor(ov, vv, sclb, ALU.mult), [dqbits, scbb], [dqval])
        am = asm[0]
        for sub in range(16):
            ps = psA if sub % 2 == 0 else psB
            P.op("tensor", lambda e, sub=sub, ps=ps: e.transpose(ps[:, 0:128], dqval[:, sub * 128:(sub + 1) * 128], identb[:]),
                 [dqval, identb], [ps])
            P.op("vector", lambda e, sub=sub, ps=ps, am=am: e.tensor_copy(am[:, sub * 128:(sub + 1) * 128], ps[:, 0:128]), [ps], [am])
        dst, key = piece_dst(-1)
        src_ap = am[:] if dst.ndim == 2 else am[:].rearrange("p (ncl h) -> p ncl h", ncl=16)
        P.dma("c_spill", out=dst, in_=src_ap, reads=[am], writes=[key])

    deq_units = []
    for nt in range(16):
        for gu, pk_t, sc_t in [(0, gate_pk, gate_sc), (1, up_pk, up_sc)]:
            def gdst(sub, nt=nt, gu=gu):
                off = nt * 524288 + gu * 262144
                blk = ffn_shard[off: off + 262144]
                return (blk.rearrange("(p c) -> p c", p=128), f"ffs#{gu}#{nt}")

            deq_units.append(lambda nt=nt, gu=gu, pk_t=pk_t, sc_t=sc_t, gdst=gdst: dequant_tile(
                pk_t[nt * 128:(nt + 1) * 128, :], sc_t[nt * 128:(nt + 1) * 128, :], nt * 2 + gu, gdst))

    for cp in range(8):
        for ht in range(2):
            def ddst(sub, ht=ht, cp=cp):
                blk = ffn_shard[2 * SH_G + cp * 524288: 2 * SH_G + (cp + 1) * 524288]
                v = blk.rearrange("(p ncl h) -> p ncl h", p=128, ncl=16)
                return (v[:, :, ht * 128:(ht + 1) * 128], f"ffs#d#{cp}#{ht}")

            deq_units.append(lambda ht=ht, cp=cp, ddst=ddst: dequant_tile(
                down_pk[ht * 128:(ht + 1) * 128, cp * 1024:(cp + 1) * 1024],
                down_sc[ht * 128:(ht + 1) * 128, cp * 64:(cp + 1) * 64], cp * 2 + ht, ddst))

    chunk_keys = {0: [], 1: [], 2: [], 3: []}
    for nt in range(16):
        for gu in range(2):
            c = (nt * 524288 + gu * 262144) // CH
            chunk_keys[c].append(f"ffs#{gu}#{nt}")
    for cp in range(8):
        c = (2 * SH_G + cp * 524288) // CH
        chunk_keys[c].append(f"ffs#d#{cp}#0")
        chunk_keys[c].append(f"ffs#d#{cp}#1")
    AG_AT = {12: 0, 24: 1, 36: 2, 48: 3}

    deq_state = {"done": 0, "gp_snap": None}

    def emit_ag(c):
        if deq_state["gp_snap"] is None:
            deq_state["gp_snap"] = P.eng["gpsimd"].count
        P.wait_keys("gpsimd", chunk_keys[c])
        P.op("gpsimd", lambda e, c=c: e.collective_compute(
            "AllGather", ALU.bypass, replica_groups=[list(range(8))],
            ins=[ffn_shard[c * CH:(c + 1) * CH].opt()], outs=[ffn_ch[c][:].opt()]),
            chunk_keys[c], [f"ffnch{c}"])

    def pop_deq(n=1):
        for _ in range(n):
            if deq_units:
                deq_units.pop(0)()
                deq_state["done"] += 1
                ag = AG_AT.get(deq_state["done"])
                if ag is not None:
                    emit_ag(ag)

    if phases < 1:
        pop_deq(len(deq_units))
        P.barrier_all(limit={"gpsimd": deq_state["gp_snap"]})
        free_group("p0")

    if phases >= 1:
        xtile = [sb(f"xtile{i}", [128, EE], F32, "p1") for i in range(2)]
        lnscr = sb("lnscr", [128, EE // 2], F32, "p1")
        ln_s = [sb(f"ln_s{i}", [128, 1], F32, "p1") for i in range(2)]
        ln_ss = [sb(f"ln_ss{i}", [128, 1], F32, "p1") for i in range(2)]
        ss2 = sb("ss2", [128, 1], F32, "p1")
        ln_mu = [sb(f"ln_mu{i}", [128, 1], F32, "p1") for i in range(2)]
        ln_m2 = [sb(f"ln_m2{i}", [128, 1], F32, "p1") for i in range(2)]
        ln_r = [sb(f"ln_r{i}", [128, 1], F32, "p1") for i in range(2)]
        xnt_tok = [sb(f"xnt_tok{i}", [128, EE], BF16, "p1") for i in range(2)]
        xnT_b = [sb(f"xnT{i}", [128, ECH * BLK], BF16, "p1") for i in range(2)]
        vtmp = sb("vtmp", [128, BLK], BF16, "p1")
        ropec = [sb(f"ropecs{i}", [64, BLK], F32, "p1") for i in range(1)]
        ropes = [sb(f"ropess{i}", [64, BLK], F32, "p1") for i in range(1)]
        rp1 = [sb(f"rp1_{i}", [64, BLK], F32, "p1") for i in range(1)]
        rp2 = [sb(f"rp2_{i}", [64, BLK], F32, "p1") for i in range(1)]
        qspill = [sb(f"qspill{i}", [128, BLK], BF16, "p1") for i in range(4)]

        nmax = sb("nmax", [1, 1], F32, "p1")
        nmax_q = sb("nmax_q", [1, 1], F32, "p1")
        nmax_k = sb("nmax_k", [1, 1], F32, "p1")
        ntmp = sb("ntmp", [1, 1], F32, "p1")

        def ln_block(x_src_rows, ti):
            par = ti % 2
            xt = xtile[ti % 2]
            sc, s_, ss, mu, m2, r_ = lnscr, ln_s[par], ln_ss[par], ln_mu[par], ln_m2[par], ln_r[par]
            xo = xnt_tok[par]
            P.dma("c_x", out=xt[:], in_=x_src_rows, writes=[xt])
            P.op("vector", lambda e, xt=xt, s_=s_: e.tensor_reduce(s_[:], xt[:], AX.X, ALU.add), [xt], [s_])
            P.op("scalar", lambda e, xt=xt, sc=sc, ss=ss: e.activation(sc[:], xt[:, 0:EE // 2], AF.Square, accum_out=ss[:]), [xt], [sc, ss])
            P.op("scalar", lambda e, xt=xt, sc=sc, ss=ss: e.activation(sc[:], xt[:, EE // 2:EE], AF.Square, accum_out=ss2[:]), [xt], [sc, "ss2"])
            P.op("vector", lambda e, ss=ss: e.tensor_tensor(ss[:], ss[:], ss2[:], ALU.add), [ss, "ss2"], [ss])
            P.op("vector", lambda e, s_=s_, mu=mu: e.tensor_scalar(mu[:], s_[:], 1.0 / EE, None, ALU.mult), [s_], [mu])
            P.op("vector", lambda e, mu=mu, m2=m2: e.tensor_scalar(m2[:], mu[:], mu[:], 1e-5, ALU.mult, ALU.subtract), [mu], [m2])
            P.op("vector", lambda e, ss=ss, m2=m2: e.scalar_tensor_tensor(m2[:], ss[:], 1.0 / EE, m2[:], ALU.mult, ALU.subtract), [ss, m2], [m2])
            P.op("scalar", lambda e, m2=m2: e.activation(m2[:], m2[:], AF.Sqrt), [m2], [m2])
            P.op("vector", lambda e, m2=m2, r_=r_: e.reciprocal(r_[:], m2[:]), [m2], [r_])
            P.op("vector", lambda e, xt=xt, xo=xo, mu=mu, r_=r_: e.tensor_scalar(xo[:], xt[:], mu[:], r_[:], ALU.subtract, ALU.mult), [xt, mu, r_], [xo])
            return xo

        trring = [psA, psB, PQ[1]]

        def transpose_to(xsrc, dst, col_base, tsub):
            for ec in range(ECH):
                ps = trring[ec % 3]
                P.op("tensor", lambda e, ec=ec, ps=ps, xsrc=xsrc: e.transpose(ps[:, 0:128], xsrc[:, ec * 128:(ec + 1) * 128], identb[:]),
                     [xsrc, identb], [ps])
                c0 = col_base(ec) + tsub * 128
                P.op("vector", lambda e, ps=ps, c0=c0, dst=dst: e.tensor_copy(dst[:, c0:c0 + 128], ps[:, 0:128]), [ps], [dst])

        def rope_apply(ps_src, pskey, dst, dst_key, c0, c1, rc, rs, par):
            r1, r2 = rp1[0], rp2[0]
            P.op("vector", lambda e, r1=r1: e.tensor_tensor(r1[:], ps_src[0:64, :], rc[0:64, :], ALU.mult), [pskey, rc], [r1])
            P.op("vector", lambda e, r2=r2: e.tensor_tensor(r2[:], ps_src[64:128, :], rs[0:64, :], ALU.mult), [pskey, rs], [r2])
            P.op("vector", lambda e, r1=r1, r2=r2: e.tensor_tensor(dst[0:64, c0:c1], r1[:], r2[:], ALU.subtract), [r1, r2], [dst_key])
            P.op("vector", lambda e, r1=r1: e.tensor_tensor(r1[:], ps_src[64:128, :], rc[0:64, :], ALU.mult), [pskey, rc], [r1])
            P.op("vector", lambda e, r2=r2: e.tensor_tensor(r2[:], ps_src[0:64, :], rs[0:64, :], ALU.mult), [pskey, rs], [r2])
            P.op("vector", lambda e, r1=r1, r2=r2: e.tensor_tensor(dst[64:128, c0:c1], r1[:], r2[:], ALU.add), [r1, r2], [dst_key])

        nsqv = vtmp
        def norm_track(src_ap, src_keys, tgt):
            P.op("vector", lambda e: e.tensor_tensor(nsqv[:], src_ap, src_ap, ALU.mult), src_keys, [nsqv])
            P.op("tensor", lambda e: e.matmul(PQF[1][0:1, :], onescol_b[:], nsqv[:], start=True, stop=True), [onescol_b, nsqv], ["pq1"])
            P.op("vector", lambda e: e.tensor_reduce(nmax[:], PQF[1][0:1, :], AX.X, ALU.max), ["pq1"], [nmax])
            P.op("vector", lambda e, tgt=tgt: e.tensor_tensor(tgt[:], tgt[:], nmax[:], ALU.max), [nmax, tgt], [tgt])

        P.op("vector", lambda e: e.memset(nmax_q[:], 0.0), [], [nmax_q])
        P.op("vector", lambda e: e.memset(nmax_k[:], 0.0), [], [nmax_k])
        psV, psV_k = PPF[1][:, 512:1024], "b3"
        psQ2, psQ2_k = PPF[1][:, 0:512], "b2"
        for tb in range(NB):
            xc = xnT_b[tb % 2]
            rc = ropec[0]
            rs = ropes[0]
            P.dma("c_rope", out=rc[:], in_=ropec_t[:, tb * BLK:(tb + 1) * BLK], writes=[rc])
            P.dma("c_rope", out=rs[:], in_=ropes_t[:, tb * BLK:(tb + 1) * BLK], writes=[rs])
            for tsub in range(4):
                r0 = tb * BLK + tsub * 128
                xo = ln_block(x_full[r0:r0 + 128, :], tb * 4 + tsub)
                transpose_to(xo, xc, lambda ec: ec * BLK, tsub)
            fns = []
            for ec in range(ECH):
                fns.append(lambda e, ec=ec, xc=xc: e.matmul(
                    psV, wvT[:, ec * 128:(ec + 1) * 128], xc[:, ec * BLK:(ec + 1) * BLK],
                    start=(ec == 0), stop=False))
            fns.append(lambda e: e.matmul(psV, bv_sb[:], onesrow_b[:], start=False, stop=True))
            P.op_group("tensor", fns, [xc, wvT, bv_sb, onesrow_b], [psV_k])
            P.op("vector", lambda e: e.tensor_copy(vtmp[:], psV), [psV_k], [vtmp])
            for tsub in range(4):
                ps = trring[tsub % 3]
                P.op("tensor", lambda e, tsub=tsub, ps=ps: e.transpose(ps[:, 0:128], vtmp[:, tsub * 128:(tsub + 1) * 128], identb[:]),
                     [vtmp, identb], [ps])
                kb_i = tb * 4 + tsub
                P.op("vector", lambda e, kb_i=kb_i, ps=ps: e.tensor_copy(V_sb[:, kb_i * 128:(kb_i + 1) * 128], ps[:, 0:128]), [ps], [V_sb])
            fns = []
            for ec in range(ECH):
                fns.append(lambda e, ec=ec, xc=xc: e.matmul(
                    psM, wkT[:, ec * 128:(ec + 1) * 128], xc[:, ec * BLK:(ec + 1) * BLK],
                    start=(ec == 0), stop=False))
            fns.append(lambda e: e.matmul(psM, bk_sb[:], onesrow_b[:], start=False, stop=True))
            P.op_group("tensor", fns, [xc, wkT, bk_sb, onesrow_b], [psM_k])
            rope_apply(psM, psM_k, K_sb, K_sb, tb * BLK, (tb + 1) * BLK, rc, rs, 0)
            for hc in range(4):
                qps, qps_k = (psU, psU_k) if hc % 2 == 0 else (psQ2, psQ2_k)
                qs = qspill[hc]
                fns = []
                for ec in range(ECH):
                    fns.append(lambda e, ec=ec, hc=hc, qps=qps, xc=xc: e.matmul(
                        qps, wqT[:, ec * 512 + hc * 128: ec * 512 + (hc + 1) * 128],
                        xc[:, ec * BLK:(ec + 1) * BLK], start=(ec == 0), stop=False))
                fns.append(lambda e, hc=hc, qps=qps: e.matmul(qps, bq_sb[0:1, hc * 128:(hc + 1) * 128], onesrow_b[:], start=False, stop=True))
                P.op_group("tensor", fns, [xc, wqT, bq_sb, onesrow_b], [qps_k])
                rope_apply(qps, qps_k, qs, qs, 0, BLK, rc, rs, hc % 2)
                P.dma("c_spill", out=qT_scr[hc * 128:(hc + 1) * 128, tb * BLK:(tb + 1) * BLK], in_=qs[:],
                      reads=[qs], writes=[f"qT#{hc}#{tb}"])
            norm_track(K_sb[:, tb * BLK:(tb + 1) * BLK], [K_sb], nmax_k)
            for hc in range(4):
                norm_track(qspill[hc][:], [qspill[hc]], nmax_q)
            pop_deq(3)
        for tb in range(4):
            xc = xnT_b[tb % 2]
            for tsub in range(4):
                r0 = tb * BLK + tsub * 128
                xo = ln_block(x_own[r0:r0 + 128, :], tb * 4 + tsub)
                transpose_to(xo, xc, lambda ec: ec * BLK, tsub)
            P.dma("c_spill", out=xn_scr[tb], in_=xc[:], reads=[xc], writes=[f"xns#{tb}"])
            pop_deq(1)
        P.op("vector", lambda e: e.tensor_tensor(ntmp[:], nmax_q[:], nmax_k[:], ALU.mult), [nmax_q, nmax_k], [ntmp])
        P.op("scalar", lambda e: e.activation(ntmp[:], ntmp[:], AF.Sqrt, scale=SCALE * SCALE), [ntmp], [ntmp])
        P.op("vector", lambda e: e.tensor_scalar(ntmp[:], ntmp[:], -1.0, None, ALU.mult), [ntmp], [ntmp])
        P.op("tensor", lambda e: e.matmul(PQF[1][:, 0:1], onesrow_f[0:1, :], ntmp[:], start=True, stop=True), [onesrow_f, ntmp], ["pq1"])
        P.op("vector", lambda e: e.tensor_copy(negB[:], PQF[1][:, 0:1]), ["pq1"], [negB])

        pop_deq(len(deq_units))

        P.barrier_all(limit={"gpsimd": deq_state["gp_snap"]})
        free_group("p1")
        free_group("p0")
        free_group("p1x")

    if phases >= 2:
        qh_buf = [sb(f"qh{i}", [128, BLK], BF16, "p23") for i in range(2)]
        pt_buf = [sb(f"pt{i}", [128, 2 * BLK], BF16, "p23") for i in range(4)]
        pacc_b = [sb(f"pacc{i}", [128, BLK], F32, "p23") for i in range(2)]
        rb_sb = sb("rb_sb", [128, BLK], F32, "p23")
        oh_sb = sb("oh_sb", [128, BLK], BF16, "p23")
        oTa = [sb(f"oTa{i}", [128, BLK], BF16, "p23") for i in range(2)]
        oTb = [sb(f"oTb{i}", [128, BLK], BF16, "p23") for i in range(2)]
        oTt = sb("oTt", [128, BLK], F32, "p23")
        oT_qt = sb("oT_qt", [128, 16 * BLK], BF16, "p23")
        woT_buf = [sb(f"woTb{i}", [128, 16 * 128], BF16, "p23") for i in range(2)]
        wgu_buf = [sb(f"wgu{i}", [128, 32 * 128], BF16, "p23") for i in range(2)]
        wd_buf = [sb(f"wdb{i}", [128, 16 * 256], BF16, "p23") for i in range(2)]
        inter = [sb(f"inter{i}", [128, BLK], BF16, "p23") for i in range(16)]
        xn_ffn = sb("xn_ffn", [128, ECH * BLK], BF16, "p23")
        gelu_s = [sb(f"gelu_s{i}", [128, BLK], F32, "p23") for i in range(2)]
        acc = sb("acc", [128, 16 * BLK], F32, "p23")
        xo_tile_b = [sb(f"xo{i}", [128, EE], F32, "p23") for i in range(1)]
        out_sb_b = [sb(f"outsb{i}", [128, EE], F32, "p23") for i in range(1)]

        psSC = PPF[0]
        psG, psG_k = PPF[1][:, 0:512], "b2"
        psU2, psU2_k = PPF[1][:, 512:1024], "b3"

        def xn_load_unit(qt):
            def f():
                P.dma("c_fw", out=xn_ffn[:], in_=xn_scr[qt], reads=[f"xns#{qt}"], writes=[xn_ffn])
            return f

        def gu_unit(qt, grp, nch):
            def f():
                wgb = wgu_buf[nch % 2]
                off = nch * 524288
                c = off // CH
                loc = grp * CH + off - c * CH
                src = ffn_ch[c][loc: loc + 524288]
                P.dma("c_fw", out=wgb[:].rearrange("p (gu c) -> p gu c", gu=2),
                      in_=src.rearrange("(gu p c) -> p gu c", gu=2, p=128),
                      reads=[f"ffnch{c}"], writes=[wgb])
                fns = []
                for ec in range(ECH):
                    fns.append(lambda e, ec=ec, wgb=wgb: e.matmul(
                        psG, wgb[:, ec * 128:(ec + 1) * 128], xn_ffn[:, ec * BLK:(ec + 1) * BLK],
                        start=(ec == 0), stop=(ec == ECH - 1)))
                P.op_group("tensor", fns, [wgb, xn_ffn], [psG_k])
                fns = []
                for ec in range(ECH):
                    fns.append(lambda e, ec=ec, wgb=wgb: e.matmul(
                        psU2, wgb[:, 2048 + ec * 128: 2048 + (ec + 1) * 128], xn_ffn[:, ec * BLK:(ec + 1) * BLK],
                        start=(ec == 0), stop=(ec == ECH - 1)))
                P.op_group("tensor", fns, [wgb, xn_ffn], [psU2_k])
                gl = gelu_s[nch % 2]
                it = inter[nch]
                P.op("scalar", lambda e, gl=gl: e.activation(gl[:], psG, AF.Gelu_apprx_tanh), [psG_k], [gl])
                P.op("vector", lambda e, it=it, gl=gl: e.tensor_tensor(it[:], gl[:], psU2, ALU.mult), [gl, psU2_k], [it])
            return f

        def down_unit(qt, grp, sh):
            def f():
                wdb = wd_buf[sh % 2]
                off = 2 * SH_G + grp * 524288
                c = off // CH
                loc = sh * CH + off - c * CH
                src = ffn_ch[c][loc: loc + 524288]
                P.dma("c_fd", out=wdb[:], in_=src.rearrange("(p c) -> p c", p=128),
                      reads=[f"ffnch{c}"], writes=[wdb])
                for hcl in range(2):
                    hc_g = sh * 2 + hcl
                    ps, pskey = (psG, psG_k) if hcl == 0 else (psU2, psU2_k)
                    fns = []
                    for ncl in range(16):
                        fns.append(lambda e, ncl=ncl, hcl=hcl, ps=ps, wdb=wdb: e.matmul(
                            ps, wdb[:, ncl * 256 + hcl * 128: ncl * 256 + (hcl + 1) * 128],
                            inter[ncl][:], start=(ncl == 0), stop=(ncl == 15)))
                    P.op_group("tensor", fns, [wdb] + [inter[n] for n in range(16)], [pskey])
                    if grp == 0:
                        P.op("vector", lambda e, hc_g=hc_g, ps=ps: e.tensor_copy(
                            acc[:, hc_g * BLK:(hc_g + 1) * BLK], ps), [pskey], [f"acc#{hc_g}"])
                    else:
                        P.op("vector", lambda e, hc_g=hc_g, ps=ps: e.tensor_tensor(
                            acc[:, hc_g * BLK:(hc_g + 1) * BLK], acc[:, hc_g * BLK:(hc_g + 1) * BLK], ps, ALU.add),
                            [pskey, f"acc#{hc_g}"], [f"acc#{hc_g}"])
            return f

        ffn_units = []
        for qt in range(4):
            ffn_units.append(xn_load_unit(qt))
            for grp in range(8):
                for nch in range(16):
                    ffn_units.append(gu_unit(qt, grp, nch))
                for sh in range(8):
                    ffn_units.append(down_unit(qt, grp, sh))
        UPQ = 1 + 8 * (16 + 8)
        ffn_state = {"i": 0}

        def pop_ffn(n, cap):
            cap = min(cap, len(ffn_units))
            while n > 0 and ffn_state["i"] < cap:
                ffn_units[ffn_state["i"]]()
                ffn_state["i"] += 1
                n -= 1

        a2a_keys = [[], []]
        pending = []
        from concourse.aot_env import allow_ici_all_to_all

        def emit_a2a(half):
            for fn in pending:
                fn()
            pending.clear()
            P.wait_keys("gpsimd", a2a_keys[half])

            def do_a2a(e, half=half):
                with allow_ici_all_to_all():
                    return e.collective_compute(
                        "AllToAll", ALU.bypass, replica_groups=[list(range(8))],
                        ins=[a2a_in[half][:].opt()], outs=[a2a_out[half][:].opt()])

            P.op("gpsimd", do_a2a, a2a_keys[half], [a2a_out[half]])
        pair_ct = {"n": 0}
        for hc in range(4):
            for qt in range(NB):
                qb = qh_buf[qt % 2]
                if qt == 0:
                    P.dma("c_q", out=qb[:], in_=qT_scr[hc * 128:(hc + 1) * 128, qt * BLK:(qt + 1) * BLK],
                          reads=[f"qT#{hc}#{qt}"], writes=[qb])
                nkc = (qt + 1) * 4
                if qt + 1 < NB:
                    nqb = qh_buf[(qt + 1) % 2]
                    P.dma("c_q", out=nqb[:], in_=qT_scr[hc * 128:(hc + 1) * 128, (qt + 1) * BLK:(qt + 2) * BLK],
                          reads=[f"qT#{hc}#{qt + 1}"], writes=[nqb])
                elif hc + 1 < 4:
                    nqb = qh_buf[0]
                    P.dma("c_q", out=nqb[:], in_=qT_scr[(hc + 1) * 128:(hc + 2) * 128, 0:BLK],
                          reads=[f"qT#{hc + 1}#0"], writes=[nqb])
                psO, psO_k = (PQF[0], "pq0") if qt % 2 == 0 else (PQF[1], "pq1")
                psD, psD_k = (PQF[2], "pq2") if qt % 2 == 0 else (PQF[3], "pq3")
                pacc = pacc_b[qt % 2]

                def emit_score(kb, qb=qb, qt=qt, pacc=pacc):
                    pt = pt_buf[(kb // 2) % 4]
                    pr = (kb // 2) % 2
                    ps = PPF[pr]
                    kA, kB = ("b0", "b1") if pr == 0 else ("b2", "b3")
                    for h in range(2):
                        P.op("tensor", lambda e, kb=kb, h=h, qb=qb, ps=ps: e.matmul(
                            ps[:, h * 512:(h + 1) * 512], K_sb[:, (kb + h) * 128:(kb + h + 1) * 128], qb[:],
                            start=True, stop=True),
                            [K_sb, qb], [kA if h == 0 else kB])
                    P.op("scalar", lambda e, pt=pt, ps=ps: e.activation(
                        pt[:], ps[:], AF.Exp, bias=negB[:], scale=SCALE),
                        [kA, kB, negB], [pt])
                    for h in range(2):
                        d = (kb + h) * 128 - qt * BLK
                        if d >= 0:
                            off = 384 - d
                            P.op("vector", lambda e, pt=pt, off=off, h=h: e.tensor_tensor(
                                pt[:, h * 512:(h + 1) * 512], pt[:, h * 512:(h + 1) * 512], maskm[:, off:off + BLK], ALU.mult),
                                [pt, maskm], [pt])
                    if kb == 0:
                        P.op("vector", lambda e, pt=pt, pacc=pacc: e.tensor_copy(pacc[:], pt[:, 0:512]), [pt], [pacc])
                    else:
                        P.op("vector", lambda e, pt=pt, pacc=pacc: e.tensor_tensor(pacc[:], pacc[:], pt[:, 0:512], ALU.add), [pt, pacc], [pacc])
                    P.op("vector", lambda e, pt=pt, pacc=pacc: e.tensor_tensor(pacc[:], pacc[:], pt[:, 512:1024], ALU.add), [pt, pacc], [pacc])

                def emit_pv(kb, nkc=nkc, psO=psO, psO_k=psO_k):
                    pt = pt_buf[(kb // 2) % 4]
                    fns = []
                    for h in range(2):
                        fns.append(lambda e, kb=kb, h=h, pt=pt, nkc=nkc, psO=psO: e.matmul(
                            psO, V_sb[:, (kb + h) * 128:(kb + h + 1) * 128],
                            pt[:, h * 512:(h + 1) * 512], start=(kb + h == 0), stop=(kb + h == nkc - 1)))
                    P.op_group("tensor", fns, [V_sb, pt], [psO_k])

                emit_score(0)
                if nkc > 2:
                    emit_score(2)
                for kb in range(0, nkc, 2):
                    if kb + 4 < nkc:
                        emit_score(kb + 4)
                    emit_pv(kb)
                    pair_ct["n"] += 1
                    if pair_ct["n"] % 24 == 0:
                        pop_ffn(8, UPQ + 17)

                def division(hc=hc, qt=qt, psO=psO, psD=psD, psD_k=psD_k, psO_k=psO_k, pacc=pacc):
                    P.op("tensor", lambda e: e.matmul(psD[:], onesmat_f[:], pacc[:], start=True, stop=True),
                         [onesmat_f, pacc], [psD_k])
                    P.op("vector", lambda e: e.reciprocal(rb_sb[:], psD[:]), [psD_k], [rb_sb])
                    P.op("vector", lambda e, psO=psO: e.tensor_tensor(oh_sb[:], psO, rb_sb[:], ALU.mult), [psO_k, rb_sb], [oh_sb])
                    qu, qc = qt // 4, qt % 4
                    half, row = hc // 2, (hc % 2) * 128
                    for piece in (qu, qu + 4):
                        key = f"a2a#{piece}#{hc}#{qt}"
                        a2a_keys[half].append(key)
                        P.dma("c_spill", out=a2a_in[half][piece, row:row + 128, qc * BLK:(qc + 1) * BLK],
                              in_=oh_sb[:], reads=[oh_sb], writes=[key])

                pending.append(division)
                if len(pending) > 1:
                    pending.pop(0)()
            if hc == 1:
                emit_a2a(0)
        emit_a2a(1)

        woT_v2 = woT_scr[:].rearrange("(oc p c) -> oc p c", oc=16, p=128)

        def do_merge(qt):
            for hd in range(16):
                rr, sub = hd // 4, hd % 4
                half, row = sub // 2, (sub % 2) * 128
                ta = oTa[hd % 2]
                tb_ = oTb[hd % 2]
                P.dma("c_oT", out=ta[:], in_=a2a_out[half][rr, row:row + 128, qt * BLK:(qt + 1) * BLK],
                      reads=[a2a_out[half]], writes=[ta])
                P.dma("c_oT", out=tb_[:], in_=a2a_out[half][rr + 4, row:row + 128, qt * BLK:(qt + 1) * BLK],
                      reads=[a2a_out[half]], writes=[tb_])
                P.op("vector", lambda e, ta=ta: e.tensor_scalar(oTt[:], ta[:], sel0[:], None, ALU.mult), [ta, sel0], [oTt])
                P.op("vector", lambda e, tb_=tb_, hd=hd: e.scalar_tensor_tensor(
                    oT_qt[:, hd * BLK:(hd + 1) * BLK], tb_[:], sel1[:], oTt[:], ALU.mult, ALU.add),
                    [tb_, sel1, oTt], [oT_qt])

        for qt in range(4):
            pop_ffn((qt + 1) * UPQ - ffn_state["i"], (qt + 1) * UPQ)
            do_merge(qt)
            pop_ffn(8, (qt + 1) * UPQ + 17)
            for oc in range(16):
                wob = woT_buf[oc % 2]
                P.dma("c_oT", out=wob[:], in_=woT_v2[oc], reads=[f"woT#{oc}"], writes=[wob])
                ps, pskey = (PQF[0], "pq0") if oc % 2 == 0 else (PQF[1], "pq1")
                fns = []
                for hd in range(16):
                    fns.append(lambda e, hd=hd, ps=ps, wob=wob: e.matmul(
                        ps, wob[:, hd * 128:(hd + 1) * 128], oT_qt[:, hd * BLK:(hd + 1) * BLK],
                        start=(hd == 0), stop=False))
                fns.append(lambda e, oc=oc, ps=ps: e.matmul(
                    ps, bo_sb[0:1, oc * 128:(oc + 1) * 128], onesrow_b[:], start=False, stop=True))
                P.op_group("tensor", fns, [wob, oT_qt, bo_sb, onesrow_b], [pskey])
                P.op("vector", lambda e, oc=oc, ps=ps: e.tensor_tensor(
                    acc[:, oc * BLK:(oc + 1) * BLK], acc[:, oc * BLK:(oc + 1) * BLK], ps, ALU.add),
                    [pskey, f"acc#{oc}"], [f"acc#{oc}"])
            pop_ffn((qt + 1) * UPQ + 17 - ffn_state["i"], (qt + 1) * UPQ + 17)
            for tsub in range(4):
                r0 = qt * BLK + tsub * 128
                xo_tile = xo_tile_b[0]
                out_sb = out_sb_b[0]
                P.dma("c_xo", out=xo_tile[:], in_=x_own[r0:r0 + 128, :], writes=[xo_tile])
                for oc in range(16):
                    trp, trk = (PQF[2], "pq2") if oc % 2 == 0 else (PQF[3], "pq3")
                    P.op("tensor", lambda e, oc=oc, tsub=tsub, trp=trp: e.transpose(
                        trp[:, 0:128], acc[:, oc * BLK + tsub * 128: oc * BLK + (tsub + 1) * 128], identf[:]),
                        [f"acc#{oc}", identf], [trk], pe_relaxed=False)
                    P.op("vector", lambda e, oc=oc, trp=trp, out_sb=out_sb, xo_tile=xo_tile: e.tensor_tensor(
                        out_sb[:, oc * 128:(oc + 1) * 128], trp[:, 0:128], xo_tile[:, oc * 128:(oc + 1) * 128], ALU.add),
                        [trk, xo_tile], [out_sb])
                P.dma("c_out", out=out_t[r0:r0 + 128, :], in_=out_sb[:], reads=[out_sb], writes=[f"out#{qt}#{tsub}"])

        P.wait_tokens("gpsimd", [P.barrier_token("c_out")])

    P.finalize()
    for h in reversed(alloc_order):
        h.__exit__(None, None, None)
    for h in reversed(groups.pop("psum", [])):
        h.__exit__(None, None, None)
    return nc



def _host_constants():
    identb = np.eye(128, dtype=np.float32).astype(ml_dtypes.bfloat16)
    identf = np.eye(128, dtype=np.float32)
    inv_freq = (1.0 / 10000.0 ** (np.arange(0, DH, 2, dtype=np.float32) / DH)).astype(np.float32)
    t = np.arange(SS, dtype=np.float32)
    ang = np.outer(inv_freq, t)
    ropec = np.cos(ang).astype(np.float32)
    ropes = np.sin(ang).astype(np.float32)
    i = np.arange(128)[:, None]
    u = np.arange(896)[None, :]
    maskm = ((u - i) >= 384).astype(np.float32).astype(ml_dtypes.bfloat16)
    return identb, identf, ropec, ropes, maskm


_CACHE = {}


def kernel(L0_attn_x, L0_attn_wq, L0_attn_bq, L0_attn_wk, L0_attn_bk,
           L0_attn_wv, L0_attn_bv, L0_attn_wo, L0_attn_bo,
           L0_ffn_gate_W_packed, L0_ffn_gate_scales,
           L0_ffn_up_W_packed, L0_ffn_up_scales,
           L0_ffn_down_W_packed, L0_ffn_down_scales,
           _phases=3):
    x = np.ascontiguousarray(np.asarray(L0_attn_x, dtype=np.float32))
    wq = np.asarray(L0_attn_wq, dtype=np.float32)
    bq = np.asarray(L0_attn_bq, dtype=np.float32)
    wk = np.asarray(L0_attn_wk, dtype=np.float32)
    bk = np.asarray(L0_attn_bk, dtype=np.float32)
    wv = np.asarray(L0_attn_wv, dtype=np.float32)
    bv = np.asarray(L0_attn_bv, dtype=np.float32)
    wo = np.asarray(L0_attn_wo, dtype=np.float32)
    bo = np.asarray(L0_attn_bo, dtype=np.float32)
    gpk = np.asarray(L0_ffn_gate_W_packed, dtype=np.int32)
    gsc = np.asarray(L0_ffn_gate_scales, dtype=np.float32)
    upk = np.asarray(L0_ffn_up_W_packed, dtype=np.int32)
    usc = np.asarray(L0_ffn_up_scales, dtype=np.float32)
    dpk = np.asarray(L0_ffn_down_W_packed, dtype=np.int32)
    dsc = np.asarray(L0_ffn_down_scales, dtype=np.float32)

    identb, identf, ropec, ropes, maskm = _host_constants()

    key = (_phases,)
    if key not in _CACHE:
        _CACHE[key] = build(phases=_phases)
    nc = _CACHE[key]

    in_maps = []
    for c in range(8):
        b, g = c // 4, c % 4
        sel0 = np.full((128, 1), 1.0 if b == 0 else 0.0, np.float32)
        in_maps.append(dict(
            x_full=x[b], x_own=np.ascontiguousarray(x[b, g * 2048:(g + 1) * 2048]),
            wq_g=np.ascontiguousarray(wq[g * 512:(g + 1) * 512]),
            bq_g=np.ascontiguousarray(bq[g * 512:(g + 1) * 512]).reshape(1, 512),
            wk_g=np.ascontiguousarray(wk[g * 128:(g + 1) * 128]),
            bk_g=np.ascontiguousarray(bk[g * 128:(g + 1) * 128]).reshape(1, 128),
            wv_g=np.ascontiguousarray(wv[g * 128:(g + 1) * 128]),
            bv_g=np.ascontiguousarray(bv[g * 128:(g + 1) * 128]).reshape(1, 128),
            wo=wo, bo=bo.reshape(1, EE),
            gate_pk=np.ascontiguousarray(gpk[c * GSH:(c + 1) * GSH]),
            gate_sc=np.ascontiguousarray(gsc[c * GSH:(c + 1) * GSH]),
            up_pk=np.ascontiguousarray(upk[c * GSH:(c + 1) * GSH]),
            up_sc=np.ascontiguousarray(usc[c * GSH:(c + 1) * GSH]),
            down_pk=np.ascontiguousarray(dpk[c * DSH:(c + 1) * DSH]),
            down_sc=np.ascontiguousarray(dsc[c * DSH:(c + 1) * DSH]),
            identb=identb, identf=identf, ropec=ropec, ropes=ropes,
            maskm=maskm, sel0=sel0, sel1=(1.0 - sel0).astype(np.float32),
        ))

    res = run_bass_kernel_spmd(nc, in_maps, core_ids=list(range(8)))
    out = np.empty((BB, SS, EE), np.float32)
    for c in range(8):
        b, g = c // 4, c % 4
        out[b, g * 2048:(g + 1) * 2048] = res.results[c]["out"]
    return out
